# version 05
# baseline (speedup 1.0000x reference)
import dataclasses
import math
import numpy as np
import concourse.bass as bass
import concourse.mybir as mybir
from concourse.bass_utils import run_bass_kernel_spmd

F32 = mybir.dt.float32
BF16 = mybir.dt.bfloat16
I32 = mybir.dt.int32
AF = mybir.ActivationFunctionType
ALU = mybir.AluOpType
AX = mybir.AxisListType

B, S, D = 4, 2048, 4096
NH, HD, NHL = 32, 128, 16
T = 1024
NC = 32
E, RK = 8, 16
SCALING = 32.0 / 16.0
SCALE_QK = float(np.float32(HD) ** np.float32(-0.5))
EPS = 1e-5
MAGIC = 0x5A17C0DE
CEXTRA = 5.0


def _alibi_slopes(n):
    base = 2.0 ** (-(2.0 ** -(math.log2(n) - 3)))
    return np.array([base ** (i + 1) for i in range(n)], dtype=np.float32)


SLOPES = _alibi_slopes(NH)


def _w_for_slope(s):
    for w in (512, 256, 128):
        if s * (w - 1) <= 136.0:
            return w
    return 128


W_HL = [_w_for_slope(float(SLOPES[hl])) for hl in range(NHL)]
NCOL_HL = [16 * (2048 // w) for w in W_HL]
ALI_OFF = np.cumsum([0] + NCOL_HL).tolist()


def _f32(x):
    return np.ascontiguousarray(x, dtype=np.float32)


def make_host_inputs(inp):
    x = np.asarray(inp["L0_attn_hidden_states"], dtype=np.float32)
    w_qkv = np.asarray(inp["L0_attn_w_qkv"], dtype=np.float32)
    b_qkv = np.asarray(inp["L0_attn_b_qkv"], dtype=np.float32)
    w_out = np.asarray(inp["L0_attn_w_out"], dtype=np.float32)
    b_out = np.asarray(inp["L0_attn_b_out"], dtype=np.float32)
    W_base = np.asarray(inp["L0_ffn_W_base"], dtype=np.float32)
    b_base = np.asarray(inp["L0_ffn_b_base"], dtype=np.float32)
    W_router = np.asarray(inp["L0_ffn_W_router"], dtype=np.float32)
    lora_A = np.asarray(inp["L0_ffn_lora_A"], dtype=np.float32)
    lora_B = np.asarray(inp["L0_ffn_lora_B"], dtype=np.float32)

    w_out_r = _f32(w_out.reshape(D, 32, 128).transpose(1, 0, 2))
    wbT_r = _f32(W_base.T.reshape(D, 32, 128).transpose(1, 0, 2))
    wrT = _f32(W_router.T)
    a2T = _f32(lora_A.reshape(E * RK, D).T)
    b2_r = _f32(lora_B.transpose(0, 2, 1).reshape(E * RK, D)
                .reshape(128, 32, 128).transpose(1, 0, 2))

    tri = np.where(np.arange(128)[:, None] > np.arange(128)[None, :],
                   np.float32(-1e9), np.float32(0.0)).astype(np.float32)
    mbr = np.zeros((E, 128), dtype=np.float32)
    for e in range(E):
        mbr[e, e * RK:(e + 1) * RK] = SCALING
    ident = np.eye(128, dtype=np.float32)

    maps = []
    for c in range(8):
        b, r = c // 2, c % 2
        hlo = NHL * r
        xT = _f32(x[b].T)
        xT_own = _f32(xT[:, r * T:(r + 1) * T])
        wq = w_qkv[:, hlo * HD:(hlo + NHL) * HD]
        wk = w_qkv[:, D + hlo * HD:D + (hlo + NHL) * HD]
        wv = w_qkv[:, 2 * D + hlo * HD:2 * D + (hlo + NHL) * HD]
        wqk_r = np.empty((NHL, D, 256), dtype=np.float32)
        for hl in range(NHL):
            wqk_r[hl, :, 0:128] = wq[:, hl * 128:(hl + 1) * 128]
            wqk_r[hl, :, 128:256] = wk[:, hl * 128:(hl + 1) * 128]
        wv_r = _f32(wv.reshape(D, 8, 256).transpose(1, 0, 2))
        bq = b_qkv[hlo * HD:(hlo + NHL) * HD]
        bk = b_qkv[D + hlo * HD:D + (hlo + NHL) * HD]
        bv = b_qkv[2 * D + hlo * HD:2 * D + (hlo + NHL) * HD]
        brow = np.empty((8, 768), dtype=np.float32)
        for hp in range(8):
            brow[hp, 0:256] = bq[hp * 256:(hp + 1) * 256]
            brow[hp, 256:512] = bk[hp * 256:(hp + 1) * 256]
            brow[hp, 512:768] = bv[hp * 256:(hp + 1) * 256]
        ali = np.empty((128, ALI_OFF[-1]), dtype=np.float32)
        p_ = np.arange(128, dtype=np.float64)
        for hl in range(NHL):
            sl = float(SLOPES[hlo + hl])
            w = W_HL[hl]
            ng = 2048 // w
            for kt in range(16):
                for g in range(ng):
                    ali[:, ALI_OFF[hl] + kt * ng + g] = (
                        sl * (128 * kt + p_ - w * g - (w - 1) / 2.0) - CEXTRA)
        p32 = np.arange(128, dtype=np.int32)
        ikv = np.empty((128, 48), dtype=np.int32)
        for kind in range(3):
            for hl in range(NHL):
                ikv[:, kind * 16 + hl] = (r * 6144 + kind * 2048
                                          + hl * 128 + p32)
        ictxw = np.empty((128, 32), dtype=np.int32)
        for hl in range(NHL):
            for th in range(2):
                ictxw[:, hl * 2 + th] = (th * 4096 + 2048 * r
                                         + 128 * hl + p32)
        ictxr = np.empty((128, 32), dtype=np.int32)
        ix2 = np.empty((128, 32), dtype=np.int32)
        for gc in range(32):
            ictxr[:, gc] = r * 4096 + 128 * gc + p32
            ix2[:, gc] = r * 4096 + 128 * gc + p32
        maps.append(dict(
            xT=xT, xT_own=xT_own, wqk_r=_f32(wqk_r), wv_r=wv_r,
            brow=_f32(brow), w_out_r=w_out_r,
            b_out_row=_f32(b_out.reshape(1, D)), wbT_r=wbT_r,
            b_base_row=_f32(b_base.reshape(1, D)), wrT=wrT, a2T=a2T,
            b2_r=b2_r, alibias=_f32(ali), tri=tri, mbr=mbr, ident=ident,
            ikv=ikv, ictxw=ictxw, ictxr=ictxr, ix2=ix2,
        ))
    return maps


def dyn(ap, off):
    return dataclasses.replace(ap, offset=off)


class Plan:
    ENGS = ("sync", "gpsimd", "tensor", "vector", "scalar")

    def __init__(self):
        self.q = {e: [] for e in self.ENGS}
        self.cnt = {}
        self.hw = {}
        self.H = {}
        self.R = {}

    def emit(self, eng, fn):
        self.q[eng].append(fn)

    def inc(self, sem, amount=16):
        self.cnt[sem] = self.cnt.get(sem, 0) + amount
        return self.cnt[sem]

    def cur(self, sem):
        return self.cnt.get(sem, 0)

    def wait(self, eng, sem, val):
        if val <= 0:
            return
        assert val <= self.cnt.get(sem, 0), (eng, sem, val, self.cnt.get(sem))
        key = (eng, sem)
        if self.hw.get(key, 0) >= val:
            return
        self.hw[key] = val
        H = self.H

        def f(e, sem=sem, val=val):
            e.wait_ge(H[sem], val)
        self.emit(eng, f)


def build_program():
    import os
    STAGE = int(os.environ.get("KSTAGE", "5"))
    nc = bass.Bass("TRN2", target_bir_lowering=False, num_devices=8)
    P_ = Plan()
    H, R = P_.H, P_.R

    def din(name, shape, dtype=F32):
        return nc.dram_tensor(name, list(shape), dtype, kind="ExternalInput").ap()

    d_xT = din("xT", (D, S))
    d_xTo = din("xT_own", (D, T))
    d_wqk = din("wqk_r", (NHL, D, 256))
    d_wv = din("wv_r", (8, D, 256))
    d_brow = din("brow", (8, 768))
    d_wout = din("w_out_r", (32, D, 128))
    d_bout = din("b_out_row", (1, D))
    d_wbT = din("wbT_r", (32, D, 128))
    d_bbase = din("b_base_row", (1, D))
    d_wrT = din("wrT", (D, 8))
    d_a2T = din("a2T", (D, 128))
    d_b2 = din("b2_r", (32, 128, 128))
    d_ali = din("alibias", (128, ALI_OFF[-1]))
    d_tri = din("tri", (128, 128))
    d_mbr = din("mbr", (E, 128))
    d_id = din("ident", (128, 128))
    d_ikv = din("ikv", (128, 48), I32)
    d_ictxw = din("ictxw", (128, 32), I32)
    d_ictxr = din("ictxr", (128, 32), I32)
    d_ix2 = din("ix2", (128, 32), I32)

    d_y = nc.dram_tensor("y", [D, T], F32, kind="ExternalOutput").ap()

    ap_kv = nc.dram_tensor("sh_kv", [2 * 3 * S, S], BF16, kind="Internal",
                           addr_space="Shared").ap()
    ap_ctx = nc.dram_tensor("sh_ctx", [2 * D, T], BF16, kind="Internal",
                            addr_space="Shared").ap()
    ap_x2 = nc.dram_tensor("sh_x2", [2 * D, T], F32, kind="Internal",
                           addr_space="Shared").ap()
    ap_fl = nc.dram_tensor("sh_fl", [1, 64], I32, kind="Internal",
                           addr_space="Shared").ap()

    from contextlib import ExitStack
    es = ExitStack()

    def sb(name, shape, dtype):
        return es.enter_context(nc.sbuf_tensor(name, list(shape), dtype))

    xln = sb("xln", (128, 65536), BF16)
    wv_sb = sb("wv_sb", (128, 8192), BF16)
    wqk_sb = sb("wqk_sb", (128, 8192), BF16)
    xf32 = sb("xf32", (128, 4096), F32)
    tmp32 = sb("tmp32", (128, 1024), F32)
    bcast = sb("bcast", (128, 2048), F32)
    rows = sb("rows", (1, 1024), F32)
    rows_b = sb("rows_b", (1, 3072), BF16)
    alib = sb("alib", (128, 512), F32)
    trit = sb("trit", (128, 128), F32)
    ones_b = sb("ones_b", (128, 512), BF16)
    brow_sb = sb("brow_sb", (1, 1536), BF16)
    brot = sb("brot", (1, 256), BF16)
    ikv_sb = sb("ikv_sb", (128, 48), I32)
    ictxw_sb = sb("ictxw_sb", (128, 32), I32)
    ictxr_sb = sb("ictxr_sb", (128, 32), I32)
    ix2_sb = sb("ix2_sb", (128, 32), I32)
    cst_eps = sb("cst_eps", (128, 1), F32)
    flag_sb = sb("flag_sb", (1, 16), I32)
    pfl_sb = sb("pfl_sb", (1, 64), I32)
    zero_sb = sb("zero_sb", (1, 16), I32)

    xb16 = xln[:, 0:4096]
    sq16 = xln[:, 4096:6144]
    cx = xln[:, 0:32768]
    att = 32768
    qb = [xln[:, att + 0:att + 2048], xln[:, att + 2048:att + 4096]]
    kb = [xln[:, att + 4096:att + 6144], xln[:, att + 6144:att + 8192]]
    vb = [xln[:, att + 8192:att + 10240], xln[:, att + 10240:att + 12288]]
    pt = [xln[:, att + 12288:att + 12800], xln[:, att + 12800:att + 13312]]
    ctxo = [xln[:, att + 13312:att + 14336], xln[:, att + 14336:att + 15360]]
    rbf = xln[:, att + 15360:att + 16384].bitcast(F32)
    tritmp = xln[:, att + 16384:att + 16896].bitcast(F32)
    x2ln = xln[:, 32768:65536]
    x8 = dataclasses.replace(xf32[:, :], offset=0).bitcast(BF16)
    vtmp = [x8[:, 0:2048], x8[:, 2048:4096]]
    qtmp = x8[:, 4096:6144]
    ktmp = x8[:, 6144:8192]
    wo_sb = wv_sb
    w32 = wqk_sb[:, :].bitcast(F32)
    x2q = [w32[:, 0:1024], w32[:, 1024:2048], w32[:, 2048:3072],
           w32[:, 3072:4096]]
    xb2 = xf32[:, :].bitcast(BF16)
    xb2a = xb2[:, 0:1024]
    xb2b = xb2[:, 1024:2048]
    g_sb = xln[:, 0:1024]
    wexp_sb = xln[:, 1024:2048]
    wr_sb = xln[:, 2048:2304]
    id_sb = xln[:, 2304:2560].bitcast(F32)
    mbr_sb = xln[0:8, 2560:2688]
    top = xln[:, 2688:2816].bitcast(F32)
    w8t_sb = xln[0:8, 2816:2944]
    b2rot = [xln[:, 2944:3072], xln[:, 3072:3200]]
    a2_sb = wv_sb[:, 0:4096]

    PS = [es.enter_context(nc.psum_tensor(f"PS{i}", [128, 512], F32))
          for i in range(8)]

    DMASEMS = ("spre", "sxa", "sxb", "sfa", "sfb", "swv", "swq", "skv",
               "skl0", "skl1", "sct0", "sct1", "sfl", "sga", "sxs0", "sxs1",
               "sxl0", "sxl1", "swo0", "swo1", "sia", "sib", "so0", "so1")
    for s in DMASEMS + ("st", "sv", "sa", "sf", "sq", "sp"):
        P_.H[s] = es.enter_context(nc.semaphore(s))


    nc.const_aps.register(F32, EPS, cst_eps[:, 0:1])

    g_, s_, t_, v_, a_ = "gpsimd", "sync", "tensor", "vector", "scalar"
    inc, cur, wait, emit = P_.inc, P_.cur, P_.wait, P_.emit

    bank_guard = [None] * 8

    def pe_take_bank(i):
        if bank_guard[i] is not None:
            wait(t_, *bank_guard[i])

    def consume_bank(i, sem, val):
        bank_guard[i] = (sem, val)

    def p0_regs(g):
        pid = g.partition_id()
        for nm in ("rank", "flw", "flw1"):
            R[nm] = g.alloc_register(nm)
        g.reg_alu(R["rank"], pid, 1, ALU.bitwise_and)
        g.reg_alu(R["flw"], R["rank"], 16, ALU.mult)
        g.reg_alu(R["flw1"], R["flw"], 8, ALU.add)
        g.memset(ones_b[:, :], 1.0)
        g.memset(cst_eps[:, :], EPS)
        g.memset(flag_sb[:, :], MAGIC)
        g.memset(zero_sb[:, :], 0)
    emit(g_, p0_regs)

    def gdma(sem, src_ap, dst_ap):
        v = inc(sem, 16)

        def f(g, src_ap=src_ap, dst_ap=dst_ap, sem=sem):
            g.dma_start(out=dst_ap, in_=src_ap).then_inc(H[sem], 16)
        emit(g_, f)
        return v

    def gdma_dyn(sem, dst_fn, src_fn):
        v = inc(sem, 16)

        def f(g, sem=sem):
            g.dma_start(out=dst_fn(g), in_=src_fn(g)).then_inc(H[sem], 16)
        emit(g_, f)
        return v

    def sdma(sem, src_ap, dst_ap):
        v = inc(sem, 16)

        def f(sy, src_ap=src_ap, dst_ap=dst_ap, sem=sem):
            sy.dma_start(out=dst_ap, in_=src_ap).then_inc(H[sem], 16)
        emit(s_, f)
        return v

    def gscat(sem, dram_ap_full, idx_sb_col, src_slice):
        v = inc(sem, 16)

        def f(g, dram_ap_full=dram_ap_full, idx_sb_col=idx_sb_col,
              src_slice=src_slice, sem=sem):
            from concourse.bass_core_rust import IndirectOffsetOnAxis
            g.indirect_dma_start(
                dram_ap_full, IndirectOffsetOnAxis(ap=idx_sb_col, axis=0),
                src_slice, None).then_inc(H[sem], 16)
        emit(g_, f)
        return v

    def ggath(sem, dst_slice, dram_ap_full, idx_sb_col):
        v = inc(sem, 16)

        def f(g, dram_ap_full=dram_ap_full, idx_sb_col=idx_sb_col,
              dst_slice=dst_slice, sem=sem):
            from concourse.bass_core_rust import IndirectOffsetOnAxis
            g.indirect_dma_start(
                dst_slice, None,
                dram_ap_full, IndirectOffsetOnAxis(ap=idx_sb_col, axis=0),
            ).then_inc(H[sem], 16)
        emit(g_, f)
        return v

    def gchunk(sem, src2d, dst_slice, width):
        return gdma(sem, src2d.rearrange("(c p) f -> p c f", p=128),
                    dst_slice.rearrange("p (c f) -> p c f", f=width))

    gdma("spre", d_tri, trit[:, :])
    gdma("spre", d_ikv, ikv_sb[:, :])
    gdma("spre", d_ictxw, ictxw_sb[:, :])
    gdma("spre", d_ictxr, ictxr_sb[:, :])
    v_pre = gdma("spre", d_ix2, ix2_sb[:, :])
    emit(g_, lambda g: g.nop().then_inc(H["sq"], 1))
    v_cst_sem = "sq"
    v_cst = inc("sq", 1)

    last_mm_for_parity = [0, 0]
    last_sq_for_parity = [0, 0]
    for dc in range(NC):
        pa = dc % 2
        if dc >= 2:
            wait(g_, "st", last_mm_for_parity[pa])
            wait(g_, "sv", last_sq_for_parity[pa])
        vld = gdma("sxa" if pa == 0 else "sxb",
                   d_xT[128 * dc:128 * (dc + 1), :],
                   xb16[:, 2048 * pa:2048 * (pa + 1)])
        wait(v_, "sxa" if pa == 0 else "sxb", vld)
        def fsq(v, pa=pa):
            v.tensor_mul(sq16[:, :], xb16[:, 2048 * pa:2048 * (pa + 1)],
                         xb16[:, 2048 * pa:2048 * (pa + 1)]).then_inc(H["sv"], 1)
        if dc >= 1:
            wait(v_, "st", last_mm_for_parity[1 - pa])
        emit(v_, fsq)
        vsq = inc("sv", 1)
        last_sq_for_parity[pa] = vsq
        wait(t_, "sxa" if pa == 0 else "sxb", vld)
        wait(t_, "sq", v_cst)
        wait(t_, "sv", vsq)
        def fmm(t, dc=dc, pa=pa):
            for ts in range(4):
                t.matmul(PS[ts][0:1, :], ones_b[:, 0:1],
                         xb16[:, 2048 * pa + 512 * ts:2048 * pa + 512 * (ts + 1)],
                         start=(dc == 0), stop=(dc == NC - 1))
            for ts in range(4):
                mm = t.matmul(PS[4 + ts][0:1, :], ones_b[:, 0:1],
                              sq16[:, 512 * ts:512 * (ts + 1)],
                              start=(dc == 0), stop=(dc == NC - 1))
            mm.then_inc(H["st"], 1)
        emit(t_, fmm)
        last_mm_for_parity[pa] = inc("st", 1)
    v_passA_mm = cur("st")

    def ln_stats_and_bcast(sem_ready_val, s1b, s2b, half_cols, nhalfpieces):
        for i, (b1, b2) in enumerate(zip(s1b, s2b)):
            wait(a_, "st", sem_ready_val)
            va0 = inc("sa", 1)
            va = inc("sa", 1)

            def f1(a, b1=b1, b2=b2, i=i, va0=va0):
                a.activation(rows[0:1, 512 * i:512 * (i + 1)],
                             PS[b1][0:1, :], AF.Copy, bias=0.0,
                             scale=1.0 / D).then_inc(H["sa"], 1)
                a.wait_ge(H["sa"], va0)
                a.activation(PS[b1][0:1, :], PS[b2][0:1, :],
                             AF.Copy, bias=0.0, scale=1.0 / D).then_inc(H["sa"], 1)
            emit(a_, f1)
            wait(v_, "sa", va)
            vv0 = inc("sv", 1)
            vv = inc("sv", 1)

            def f2(v, b1=b1, b2=b2, i=i, vv0=vv0):
                v.tensor_mul(tmp32[0:1, 512 * i:512 * (i + 1)],
                             rows[0:1, 512 * i:512 * (i + 1)],
                             rows[0:1, 512 * i:512 * (i + 1)]).then_inc(H["sv"], 1)
                v.wait_ge(H["sv"], vv0)
                v.scalar_tensor_tensor(PS[b2][0:1, :],
                                       tmp32[0:1, 512 * i:512 * (i + 1)],
                                       -1.0, PS[b1][0:1, :],
                                       ALU.mult, ALU.add).then_inc(H["sv"], 1)
            emit(v_, f2)
            wait(a_, "sv", vv)
            vb0 = inc("sa", 1)
            vb1 = inc("sa", 1)
            va2 = inc("sa", 1)

            def f3(a, b1=b1, b2=b2, i=i, vb0=vb0, vb1=vb1):
                a.activation(PS[b1][0:1, :], PS[b2][0:1, :], AF.Ln,
                             bias=EPS, scale=1.0).then_inc(H["sa"], 1)
                a.wait_ge(H["sa"], vb0)
                a.activation(tmp32[0:1, 512 * i:512 * (i + 1)],
                             PS[b1][0:1, :], AF.Exp, bias=0.0,
                             scale=-0.5).then_inc(H["sa"], 1)
                a.wait_ge(H["sa"], vb1)
                a.activation(rows_b[0:1, 512 * i:512 * (i + 1)],
                             tmp32[0:1, 512 * i:512 * (i + 1)],
                             AF.Copy, bias=0.0, scale=1.0).then_inc(H["sa"], 1)
            emit(a_, f3)
            wait(v_, "sa", va2)
            def f4(v, i=i):
                v.tensor_sub(rows_b[0:1, 1024 + 512 * i:1024 + 512 * (i + 1)],
                             tmp32[0:1, 512 * i:512 * (i + 1)],
                             rows_b[0:1, 512 * i:512 * (i + 1)])
                v.tensor_mul(rows_b[0:1, 2048 + 512 * i:2048 + 512 * (i + 1)],
                             rows[0:1, 512 * i:512 * (i + 1)],
                             tmp32[0:1, 512 * i:512 * (i + 1)]).then_inc(H["sv"], 1)
            emit(v_, f4)
            vlo = inc("sv", 1)
            wait(t_, "sv", vlo)
            def f5(t, b1=b1, b2=b2, i=i):
                t.matmul(PS[b1][:, :], ones_b[0:1, 0:128],
                         rows_b[0:1, 512 * i:512 * (i + 1)], start=True, stop=False)
                t.matmul(PS[b1][:, :], ones_b[0:1, 0:128],
                         rows_b[0:1, 1024 + 512 * i:1024 + 512 * (i + 1)],
                         start=False, stop=True)
                t.matmul(PS[b2][:, :], ones_b[0:1, 0:128],
                         rows_b[0:1, 2048 + 512 * i:2048 + 512 * (i + 1)],
                         start=True, stop=True).then_inc(H["st"], 1)
            emit(t_, f5)
            vmm = inc("st", 1)
            wait(a_, "st", vmm)
            def f6(a, b1=b1, b2=b2, i=i):
                a.activation(bcast[:, 512 * i:512 * (i + 1)], PS[b1][:, :],
                             AF.Copy, bias=0.0, scale=1.0)
                a.activation(bcast[:, 1024 + 512 * i:1024 + 512 * (i + 1)],
                             PS[b2][:, :], AF.Copy, bias=0.0,
                             scale=1.0).then_inc(H["sa"], 1)
            emit(a_, f6)
            v_f6 = inc("sa", 1)
            consume_bank(b1, "sa", v_f6)
            consume_bank(b2, "sa", v_f6)
        return cur("sa")

    xln_written = []
    for h in range(2):
        if h == 1:
            wait(a_, "sv", cur("sv"))
            wait(v_, "sv", cur("sv"))
            wait(t_, "sv", cur("sv"))
        ready = ln_stats_and_bcast(v_passA_mm, [2 * h, 2 * h + 1],
                                   [4 + 2 * h, 4 + 2 * h + 1], None, None)
        lastv = [cur("sv"), cur("sv")]
        for dc in range(NC):
            pa = dc % 2
            wait(s_, "sv", lastv[pa])
            vld = sdma("sfa" if pa == 0 else "sfb",
                       d_xT[128 * dc:128 * (dc + 1), 1024 * h:1024 * (h + 1)],
                       xf32[:, 2048 * pa:2048 * pa + 1024])
            wait(v_, "sfa" if pa == 0 else "sfb", vld)
            wait(v_, "sv", cur("sv"))
            wait(v_, "sa", ready)
            vm0 = inc("sv", 1)
            lastv[pa] = inc("sv", 1)

            def fap(v, dc=dc, pa=pa, h=h, vm0=vm0):
                v.tensor_mul(tmp32[:, :],
                             xf32[:, 2048 * pa:2048 * pa + 1024],
                             bcast[:, 0:1024]).then_inc(H["sv"], 1)
                v.wait_ge(H["sv"], vm0)
                v.tensor_sub(xln[:, 2048 * dc + 1024 * h:2048 * dc + 1024 * (h + 1)],
                             tmp32[:, :], bcast[:, 1024:2048]).then_inc(H["sv"], 1)
            emit(v_, fap)
        xln_written.append(cur("sv"))
    v_ln1_done = cur("sv")

    def dump_and_finish(src_getter, nchunks, chunk_cols, src_f32=False):
        for dc in range(nchunks):
            srcap = src_getter(dc)
            for sem in ("sv", "sa", "st") + DMASEMS:
                wait(v_, sem, cur(sem))
            pa = dc % 2
            def fdp(v, srcap=srcap, pa=pa, chunk_cols=chunk_cols):
                v.tensor_copy(x2q[2 + pa][:, 0:chunk_cols],
                              srcap).then_inc(H["sv"], 1)
            emit(v_, fdp)
            vv = inc("sv", 1)
            wait(s_, "sv", vv)
            def fds(sy, dc=dc, pa=pa, chunk_cols=chunk_cols):
                sy.dma_start(out=d_y[128 * dc:128 * (dc + 1), 0:chunk_cols],
                             in_=x2q[2 + pa][:, 0:chunk_cols]).then_inc(H["so0"], 16)
            emit(s_, fds)
            inc("so0", 16)
        for sem in DMASEMS:
            wait(s_, sem, cur(sem))

    if STAGE == 1:
        dump_and_finish(lambda dc: xln[:, 2048 * dc:2048 * dc + 1024], 32, 1024)
        return _assemble(nc, P_, es)

    v_brow = [0] * 8
    v_wv = [0] * 8
    v_wqk = [0] * NHL
    kv_spilled = []
    pe_wv_done = [0] * 8
    pe_wqk_done = [0] * NHL

    v_pair_spill = [0] * 9
    v_qk_spill = [0] * (NHL + 1)
    for hp in range(8):
        if hp >= 1:
            wait(g_, "st", pe_wv_done[hp - 1])
        if hp >= 2:
            wait(g_, "st", pe_wqk_done[2 * (hp - 2) + 1])
        for q4 in range(4):
            v_wv[hp] = gchunk(
                "swv", d_wv[hp, 1024 * q4:1024 * (q4 + 1), :],
                wv_sb[:, 2048 * q4:2048 * (q4 + 1)], 256)
        v_brow[hp] = gdma("swv", d_brow[hp:hp + 1, :],
                          brow_sb[0:1, 768 * (hp % 2):768 * (hp % 2) + 768])
        wait(t_, "swv", v_brow[hp])
        wait(t_, "sv", v_ln1_done)
        evac_done = [0, 0]
        for tt in range(16):
            bk = tt % 2
            pe_take_bank(bk)
            def fv(t, tt=tt, bk=bk, hp=hp):
                for dc in range(NC):
                    t.matmul(PS[bk][:, 0:256],
                             xln[:, 2048 * dc + 128 * tt:2048 * dc + 128 * (tt + 1)],
                             wv_sb[:, 256 * dc:256 * (dc + 1)],
                             start=(dc == 0), stop=False)
                t.matmul(PS[bk][:, 0:256], ones_b[0:1, 0:128],
                         brow_sb[0:1, 768 * (hp % 2) + 512:768 * (hp % 2) + 768],
                         start=False, stop=True).then_inc(H["st"], 1)
            emit(t_, fv)
            vmm = inc("st", 1)
            wait(a_, "st", vmm)
            if hp >= 1:
                wait(a_, "skv", cur("skv"))
            def fe(a, tt=tt, bk=bk):
                a.activation(vtmp[0][:, 128 * tt:128 * (tt + 1)],
                             PS[bk][:, 0:128], AF.Copy, bias=0.0, scale=1.0)
                a.activation(vtmp[1][:, 128 * tt:128 * (tt + 1)],
                             PS[bk][:, 128:256], AF.Copy, bias=0.0,
                             scale=1.0).then_inc(H["sa"], 1)
            emit(a_, fe)
            va = inc("sa", 1)
            consume_bank(bk, "sa", va)
            evac_done[bk] = va
        pe_wv_done[hp] = cur("st")
        wait(g_, "sa", max(evac_done))
        wait(g_, "spre", v_pre)
        for ho in range(2):
            hl = 2 * hp + ho
            kv_spilled.append(gscat("skv", ap_kv, ikv_sb[:, 32 + hl:33 + hl],
                                    vtmp[ho][:, :]))
        v_pair_spill[hp] = cur("skv")

        for ho in range(2):
            hl = 2 * hp + ho
            if hl >= 1:
                wait(g_, "st", pe_wqk_done[hl - 1])
            for q4 in range(4):
                v_wqk[hl] = gchunk(
                    "swq", d_wqk[hl, 1024 * q4:1024 * (q4 + 1), :],
                    wqk_sb[:, 2048 * q4:2048 * (q4 + 1)], 256)
            wait(t_, "swq", v_wqk[hl])
            for kind, buf, sc, boff in ((0, qtmp, SCALE_QK, 128 * ho),
                                        (1, ktmp, 1.0, 256 + 128 * ho)):
                ev = 0
                for ts in range(4):
                    bk = 2 + ts % 2
                    pe_take_bank(bk)
                    def fq(t, ts=ts, bk=bk, kind=kind, hp=hp, boff=boff, ho=ho):
                        for dc in range(NC):
                            t.matmul(PS[bk][:, :],
                                     wqk_sb[:, 256 * dc + 128 * kind:
                                            256 * dc + 128 * (kind + 1)],
                                     xln[:, 2048 * dc + 512 * ts:
                                         2048 * dc + 512 * (ts + 1)],
                                     start=(dc == 0), stop=False)
                        t.matmul(PS[bk][:, :],
                                 brow_sb[0:1, 768 * (hp % 2) + boff:
                                         768 * (hp % 2) + boff + 128],
                                 ones_b[0:1, 0:512],
                                 start=False, stop=True).then_inc(H["st"], 1)
                    emit(t_, fq)
                    vmm = inc("st", 1)
                    wait(a_, "st", vmm)
                    if hl >= 1:
                        wait(a_, "skv", cur("skv"))
                    def fqe(a, ts=ts, bk=bk, buf=buf, sc=sc):
                        a.activation(buf[:, 512 * ts:512 * (ts + 1)], PS[bk][:, :],
                                     AF.Copy, bias=0.0, scale=sc).then_inc(H["sa"], 1)
                    emit(a_, fqe)
                    va = inc("sa", 1)
                    consume_bank(bk, "sa", va)
                    ev = va
                wait(g_, "sa", ev)
                kv_spilled.append(gscat(
                    "skv", ap_kv, ikv_sb[:, 16 * kind + hl:16 * kind + hl + 1],
                    buf[:, :]))
            v_qk_spill[hl] = cur("skv")
            pe_wqk_done[hl] = cur("st")
    v_qkv_pe_done = cur("st")
    v_kv_spill_done = cur("skv")

    if STAGE == 2:
        wait(g_, "skv", v_kv_spill_done)
        for hl in range(NHL):
            ggath("sga", cx[:, 2048 * hl:2048 * (hl + 1)], ap_kv,
                  ikv_sb[:, 16 + hl:17 + hl])
        wait(v_, "sga", cur("sga"))
        dump_and_finish(lambda dc: cx[:, 2048 * dc:2048 * dc + 1024], 16, 1024)
        return _assemble(nc, P_, es)

    def phase_barrier():
        for eng in Plan.ENGS:
            for sem in DMASEMS + ("st", "sv", "sa", "sf"):
                if cur(sem):
                    wait(eng, sem, cur(sem))

    v_wo = [0, 0]
    v_brot = [0, 0]
    v_x2in = [0, 0]

    def plan_oproj_loads(oc):
        pa = oc % 2
        sem_wo = "swo0" if pa == 0 else "swo1"
        for q4 in range(4):
            v_wo[pa] = gchunk(
                sem_wo, d_wout[oc, 1024 * q4:1024 * (q4 + 1), :],
                wo_sb[:, 4096 * pa + 1024 * q4:4096 * pa + 1024 * (q4 + 1)],
                128)
        v_brot[pa] = gdma(sem_wo, d_bout[0:1, 128 * oc:128 * (oc + 1)],
                          brot[0:1, 128 * pa:128 * (pa + 1)])
        v_x2in[pa] = sdma("sia" if pa == 0 else "sib",
                          d_xTo[128 * oc:128 * (oc + 1), :], x2q[pa])

    wait(g_, "st", pe_wv_done[7])
    wait(g_, "st", pe_wqk_done[NHL - 1])
    wait(s_, "st", pe_wqk_done[NHL - 1])
    plan_oproj_loads(0)
    plan_oproj_loads(1)

    POLLST = {"regs": None}

    def plan_poll(rnd, sfval):
        def fpoll(sy, rnd=rnd):
            if POLLST["regs"] is None:
                POLLST["regs"] = dict(
                    ne=sy.alloc_register("p_ne"), v0=sy.alloc_register("p_v0"),
                    v1=sy.alloc_register("p_v1"), t=sy.alloc_register("p_t"),
                    po=sy.alloc_register("p_po"))
                Rp = POLLST["regs"]
                sy.reg_alu(Rp["po"], sy.partition_id(), 1, ALU.bitwise_and)
                sy.reg_alu(Rp["po"], Rp["po"], 1, ALU.bitwise_xor)
                sy.reg_alu(Rp["po"], Rp["po"], 16, ALU.mult)
                sy.reg_mov(Rp["t"], 16)
            Rp = POLLST["regs"]
            c0, c1 = (0, 16) if rnd == 0 else (8, 24)
            sy.reg_mov(Rp["ne"], 1)
            with sy.While(Rp["ne"]):
                sy.dma_start(out=pfl_sb[0:1, :],
                             in_=ap_fl).then_inc(H["sp"], 16)
                sy.wait_ge(H["sp"], Rp["t"])
                sy.reg_add(Rp["t"], Rp["t"], 16)
                sy.reg_load(Rp["v0"], pfl_sb[0:1, c0:c0 + 1])
                sy.reg_load(Rp["v1"], pfl_sb[0:1, c1:c1 + 1])
                sy.reg_alu(Rp["v0"], Rp["v0"], MAGIC, ALU.not_equal)
                sy.reg_alu(Rp["v1"], Rp["v1"], MAGIC, ALU.not_equal)
                sy.reg_alu(Rp["ne"], Rp["v0"], Rp["v1"], ALU.bitwise_or)
            if rnd == 1:
                sy.reg_sub(Rp["t"], Rp["t"], 16)
                sy.wait_ge(H["sp"], Rp["t"])
                sy.dma_start(out=dyn(ap_fl[0:1, 0:16], Rp["po"]),
                             in_=zero_sb[0:1, :]).then_inc(H["sfl"], 16)
            sy.nop().then_inc(H["sf"], 1)
        emit(s_, fpoll)
        if rnd == 1:
            inc("sfl", 16)
        assert inc("sf", 1) == sfval

    v_qkvld = [0] * NHL
    pe_head_done = [0] * NHL
    last_scat = 0

    v_alild = [0] * NHL

    def plan_kv_load(hl):
        pa = hl % 2
        sem = "skl0" if pa == 0 else "skl1"
        if hl >= 2:
            wait(g_, "st", pe_head_done[hl - 2])
            wait(g_, "sa", cur("sa"))
        wait(g_, "skv", v_kv_spill_done)
        ncol = NCOL_HL[hl]
        v_alild[hl] = gdma(sem, d_ali[:, ALI_OFF[hl]:ALI_OFF[hl] + ncol],
                           alib[:, 256 * pa:256 * pa + ncol])
        for kind, buf in ((0, qb[pa]), (1, kb[pa]), (2, vb[pa])):
            ggath(sem, buf[:, :], ap_kv,
                  ikv_sb[:, 16 * kind + hl:16 * kind + hl + 1])
        return cur(sem)

    v_qkvld[0] = plan_kv_load(0)
    v_qkvld[1] = plan_kv_load(1)

    pt_consumed = [0, 0]
    ctxo_dma = [0, 0]

    for hl in range(NHL):
        pa = hl % 2
        w = W_HL[hl]
        ng = 2048 // w
        for j in range(4):
            kts = list(range(4 * j + 4))
            for kt in kts:
                diag = kt >= 4 * j
                m = kt - 4 * j if diag else 0
                npr = 512 - 128 * m
                c0g = 512 * j + 128 * m
                bk = kt % 2
                pe_take_bank(bk)
                wait(t_, "skl0" if pa == 0 else "skl1", v_qkvld[hl])
                def fsc(t, bk=bk, kt=kt, npr=npr, c0g=c0g, j=j, pa=pa):
                    t.matmul(PS[bk][:, 0:npr],
                             kb[pa][:, 128 * kt:128 * (kt + 1)],
                             qb[pa][:, c0g:512 * (j + 1)],
                             start=True, stop=True).then_inc(H["st"], 1)
                emit(t_, fsc)
                v_sc = inc("st", 1)
                v_tri_v = 0
                if diag:
                    wait(v_, "st", v_sc)
                    wait(v_, "spre", v_pre)
                    def ftr(v, bk=bk):
                        v.tensor_add(tritmp[:, 0:128], PS[bk][:, 0:128],
                                     trit[:, :]).then_inc(H["sv"], 1)
                    emit(v_, ftr)
                    v_tri_v = inc("sv", 1)
                wait(a_, "st", max(v_sc, pt_consumed[bk]))
                if diag:
                    wait(a_, "sv", v_tri_v)
                wait(a_, "skl0" if pa == 0 else "skl1", v_qkvld[hl])
                calls = []
                a0 = c0g
                while a0 < 512 * (j + 1):
                    gg = a0 // w
                    b0 = min((gg + 1) * w, 512 * (j + 1))
                    if diag and a0 == c0g:
                        b0 = min(b0, c0g + 128)
                    calls.append((a0, b0, gg))
                    a0 = b0
                def fex(a, calls=calls, bk=bk, diag=diag, c0g=c0g, hl=hl,
                        kt=kt, ng=ng, pa=pa):
                    for (a0, b0, gg) in calls:
                        colb = 256 * pa + kt * ng + gg
                        if diag and a0 == c0g:
                            src = tritmp[:, 0:b0 - a0]
                        else:
                            src = PS[bk][:, a0 - c0g:b0 - c0g]
                        ai = a.activation(pt[bk][:, a0 - c0g:b0 - c0g], src,
                                          AF.Exp,
                                          bias=alib[:, colb:colb + 1],
                                          scale=1.0)
                    ai.then_inc(H["sa"], 1)
                emit(a_, fex)
                v_ex = inc("sa", 1)
                consume_bank(bk, "sa", v_ex)
                pe_take_bank(4) if kt == 0 else None
                pe_take_bank(5) if kt == 0 else None
                wait(t_, "sa", v_ex)
                last = (kt == kts[-1])
                def fcd(t, bk=bk, kt=kt, npr=npr, c0g=c0g, j=j, pa=pa,
                        last=last):
                    t.matmul(PS[4][:, c0g - 512 * j:512],
                             vb[pa][:, 128 * kt:128 * (kt + 1)],
                             pt[bk][:, 0:npr],
                             start=(kt == 0), stop=last)
                    t.matmul(PS[5][0:1, c0g - 512 * j:512],
                             ones_b[:, 0:1], pt[bk][:, 0:npr],
                             start=(kt == 0), stop=last).then_inc(H["st"], 1)
                emit(t_, fcd)
                pt_consumed[bk] = inc("st", 1)
            v_cd = cur("st")
            wait(v_, "st", v_cd)
            def fr1(v):
                v.reciprocal(rows[0:1, 0:512], PS[5][0:1, :]).then_inc(H["sv"], 1)
            emit(v_, fr1)
            v_rec = inc("sv", 1)
            consume_bank(5, "sv", v_rec)
            wait(a_, "sv", v_rec)
            def fr2(a):
                a.activation(rows_b[0:1, 0:512], rows[0:1, 0:512], AF.Copy,
                             bias=0.0, scale=1.0).then_inc(H["sa"], 1)
            emit(a_, fr2)
            v_hi = inc("sa", 1)
            wait(v_, "sa", v_hi)
            def fr3(v):
                v.tensor_sub(rows_b[0:1, 512:1024], rows[0:1, 0:512],
                             rows_b[0:1, 0:512]).then_inc(H["sv"], 1)
            emit(v_, fr3)
            v_lo = inc("sv", 1)
            pe_take_bank(6)
            wait(t_, "sv", v_lo)
            def fr4(t):
                t.matmul(PS[6][:, :], ones_b[0:1, 0:128], rows_b[0:1, 0:512],
                         start=True, stop=False)
                t.matmul(PS[6][:, :], ones_b[0:1, 0:128], rows_b[0:1, 512:1024],
                         start=False, stop=True).then_inc(H["st"], 1)
            emit(t_, fr4)
            v_bc = inc("st", 1)
            wait(a_, "st", v_bc)
            def fr5(a):
                a.activation(rbf[:, :], PS[6][:, :], AF.Copy, bias=0.0,
                             scale=1.0).then_inc(H["sa"], 1)
            emit(a_, fr5)
            v_rb = inc("sa", 1)
            consume_bank(6, "sa", v_rb)
            thalf = j // 2
            jh = j % 2
            wait(v_, "sa", v_rb)
            wait(v_, "sct0" if thalf == 0 else "sct1", ctxo_dma[thalf])
            def fr6(v, thalf=thalf, jh=jh):
                v.tensor_mul(ctxo[thalf][:, 512 * jh:512 * (jh + 1)],
                             PS[4][:, :], rbf[:, :]).then_inc(H["sv"], 1)
            emit(v_, fr6)
            v_cm = inc("sv", 1)
            consume_bank(4, "sv", v_cm)
            if jh == 1:
                wait(g_, "sv", v_cm)
                ctxo_dma[thalf] = gscat(
                    "sct0" if thalf == 0 else "sct1", ap_ctx,
                    ictxw_sb[:, hl * 2 + thalf:hl * 2 + thalf + 1],
                    ctxo[thalf][:, :])
        pe_head_done[hl] = cur("st")
        if hl + 2 < NHL:
            v_qkvld[hl + 2] = plan_kv_load(hl + 2)
        if hl == 7:
            wait(g_, "sct0", cur("sct0"))
            wait(g_, "sct1", cur("sct1"))
            gdma_dyn("sfl", lambda g: dyn(ap_fl[0:1, 0:8], R["flw"]),
                     lambda g: flag_sb[0:1, 0:8])
            plan_poll(0, 1)
            wait(g_, "sf", 1)
            for gc in list(range(0, 8)) + list(range(16, 24)):
                ggath("sga", cx[:, 1024 * gc:1024 * (gc + 1)], ap_ctx,
                      ictxr_sb[:, gc:gc + 1])

    wait(g_, "sct0", cur("sct0"))
    wait(g_, "sct1", cur("sct1"))
    v_flag = gdma_dyn(
        "sfl",
        lambda g: dyn(ap_fl[0:1, 8:16], R["flw1"]),
        lambda g: flag_sb[0:1, 0:8])

    plan_poll(1, 2)
    v_poll = cur("sf")

    wait(g_, "sf", v_poll)
    for gc in list(range(8, 16)) + list(range(24, 32)):
        ggath("sga", cx[:, 1024 * gc:1024 * (gc + 1)], ap_ctx,
              ictxr_sb[:, gc:gc + 1])
    v_gather = cur("sga")

    if STAGE == 3:
        wait(v_, "sga", v_gather)
        dump_and_finish(lambda dc: cx[:, 1024 * dc:1024 * (dc + 1)], 32, 1024)
        return _assemble(nc, P_, es)

    phase_barrier()
    wo_pe_done = [0, 0]
    x2out_dma = [0, 0]
    v_x2add = [0, 0]
    for oc in range(32):
        pa = oc % 2
        sem_wo = "swo0" if pa == 0 else "swo1"
        if oc >= 2:
            wait(g_, "st", wo_pe_done[pa])
            wait(s_, "sv", v_x2add[pa])
            plan_oproj_loads(oc)
        wait(t_, sem_wo, v_brot[pa])
        wait(t_, "sga", v_gather)
        for th in range(2):
            pe_take_bank(th)
            def fop(t, th=th, pa=pa):
                for fc in range(32):
                    t.matmul(PS[th][:, :],
                             wo_sb[:, 4096 * pa + 128 * fc:4096 * pa + 128 * (fc + 1)],
                             cx[:, 1024 * fc + 512 * th:1024 * fc + 512 * (th + 1)],
                             start=(fc == 0), stop=False)
                t.matmul(PS[th][:, :], brot[0:1, 128 * pa:128 * (pa + 1)],
                         ones_b[0:1, 0:512],
                         start=False, stop=True).then_inc(H["st"], 1)
            emit(t_, fop)
            inc("st", 1)
        wo_pe_done[pa] = cur("st")
        wait(v_, "sia" if pa == 0 else "sib", v_x2in[pa])
        wait(v_, "st", wo_pe_done[pa])
        wait(v_, "sxs0" if pa == 0 else "sxs1", x2out_dma[pa])
        def fx2(v, pa=pa):
            v.tensor_add(x2q[2 + pa][:, 0:512], x2q[pa][:, 0:512], PS[0][:, :])
            v.tensor_add(x2q[2 + pa][:, 512:1024], x2q[pa][:, 512:1024],
                         PS[1][:, :]).then_inc(H["sv"], 1)
        emit(v_, fx2)
        v_add = inc("sv", 1)
        v_x2add[pa] = v_add
        consume_bank(0, "sv", v_add)
        consume_bank(1, "sv", v_add)
        wait(a_, "sv", v_add)
        def fcb(a, pa=pa):
            a.activation(xb2a[:, :], x2q[2 + pa][:, :], AF.Copy, bias=0.0,
                         scale=1.0).then_inc(H["sa"], 1)
        emit(a_, fcb)
        v_c1 = inc("sa", 1)
        wait(v_, "sa", v_c1)
        def fsq2(v):
            v.tensor_mul(xb2b[:, :], xb2a[:, :], xb2a[:, :]).then_inc(H["sv"], 1)
        emit(v_, fsq2)
        v_s2 = inc("sv", 1)
        if oc == 0:
            pe_take_bank(4), pe_take_bank(5), pe_take_bank(6), pe_take_bank(7)
        wait(t_, "sv", v_s2)
        def fst(t, oc=oc):
            for th2 in range(2):
                t.matmul(PS[4 + th2][0:1, :], ones_b[:, 0:1],
                         xb2a[:, 512 * th2:512 * (th2 + 1)],
                         start=(oc == 0), stop=(oc == 31))
                mm = t.matmul(PS[6 + th2][0:1, :], ones_b[:, 0:1],
                              xb2b[:, 512 * th2:512 * (th2 + 1)],
                              start=(oc == 0), stop=(oc == 31))
            mm.then_inc(H["st"], 1)
        emit(t_, fst)
        v_stm = inc("st", 1)
        wait(g_, "sv", v_add)
        x2out_dma[pa] = gscat("sxs0" if pa == 0 else "sxs1",
                              ap_x2, ix2_sb[:, oc:oc + 1],
                              x2q[2 + pa][:, :])
    v_x2s0, v_x2s1 = cur("sxs0"), cur("sxs1")
    v_ln2_mm = cur("st")

    wait(a_, "sv", cur("sv"))
    ln_stats_and_bcast(v_ln2_mm, [4, 5], [6, 7], None, None)
    v_ln2_ready = cur("sa")

    x2ln_parts = [0, 0]
    for dc in range(32):
        pa = dc % 2
        wait(g_, "sxs0", v_x2s0)
        wait(g_, "sxs1", v_x2s1)
        if dc >= 2:
            wait(g_, "sv", x2ln_parts[pa])
        vld = ggath("sxl0" if pa == 0 else "sxl1",
                    x2q[pa][:, :], ap_x2, ix2_sb[:, dc:dc + 1])
        wait(v_, "sxl0" if pa == 0 else "sxl1", vld)
        wait(v_, "sv", cur("sv"))
        wait(v_, "sa", v_ln2_ready)
        vm0 = inc("sv", 1)
        x2ln_parts[pa] = inc("sv", 1)

        def fl2(v, dc=dc, pa=pa, vm0=vm0):
            v.tensor_mul(tmp32[:, :], x2q[pa][:, :],
                         bcast[:, 0:1024]).then_inc(H["sv"], 1)
            v.wait_ge(H["sv"], vm0)
            v.tensor_sub(x2ln[:, 1024 * dc:1024 * (dc + 1)], tmp32[:, :],
                         bcast[:, 1024:2048]).then_inc(H["sv"], 1)
        emit(v_, fl2)
    v_x2ln_done = cur("sv")

    if STAGE == 4:
        wait(v_, "sv", v_x2ln_done)
        dump_and_finish(lambda dc: x2ln[:, 1024 * dc:1024 * (dc + 1)], 32, 1024)
        return _assemble(nc, P_, es)

    phase_barrier()
    gchunk("spre", d_wrT, wr_sb, 8)
    gdma("spre", d_id, id_sb)
    gdma("spre", d_mbr, mbr_sb)
    v_pre4 = gchunk("spre", d_a2T, a2_sb, 128)

    for tt in range(8):
        pe_take_bank(2)
        wait(t_, "spre", v_pre4)
        wait(t_, "sv", v_x2ln_done)
        def flg(t, tt=tt):
            for dc in range(NC):
                mm = t.matmul(
                    PS[2][:, 0:8],
                    x2ln[:, 1024 * dc + 128 * tt:1024 * dc + 128 * (tt + 1)],
                    wr_sb[:, 8 * dc:8 * (dc + 1)],
                    start=(dc == 0), stop=(dc == NC - 1))
            mm.then_inc(H["st"], 1)
        emit(t_, flg)
        v_lg = inc("st", 1)
        wait(a_, "st", v_lg)
        def fle(a):
            a.activation(top[:, 0:8], PS[2][:, 0:8], AF.Copy, bias=0.0,
                         scale=1.0).then_inc(H["sa"], 1)
        emit(a_, fle)
        v_le = inc("sa", 1)
        consume_bank(2, "sa", v_le)
        wait(v_, "sa", v_le)
        w1 = inc("sv", 1)
        w2 = inc("sv", 1)
        w3 = inc("sv", 1)
        w4 = inc("sv", 1)
        w5 = inc("sv", 1)
        v_t1 = inc("sv", 1)

        def ftp1(v, w1=w1, w2=w2, w3=w3, w4=w4, w5=w5):
            v.reduce_max(top[:, 8:9], top[:, 0:8], AX.X).then_inc(H["sv"], 1)
            v.wait_ge(H["sv"], w1)
            v.tensor_scalar(top[:, 16:24], top[:, 0:8], top[:, 8:9],
                            None, ALU.is_ge).then_inc(H["sv"], 1)
            v.wait_ge(H["sv"], w2)
            v.scalar_tensor_tensor(top[:, 24:32], top[:, 16:24], -1e30,
                                   top[:, 0:8], ALU.mult,
                                   ALU.add).then_inc(H["sv"], 1)
            v.wait_ge(H["sv"], w3)
            v.reduce_max(top[:, 9:10], top[:, 24:32],
                         AX.X).then_inc(H["sv"], 1)
            v.wait_ge(H["sv"], w4)
            v.tensor_scalar(top[:, 32:40], top[:, 24:32], top[:, 9:10], None,
                            ALU.is_ge).then_inc(H["sv"], 1)
            v.wait_ge(H["sv"], w5)
            v.tensor_sub(top[:, 10:11], top[:, 9:10],
                         top[:, 8:9]).then_inc(H["sv"], 1)
        emit(v_, ftp1)
        wait(a_, "sv", v_t1)
        def ftp2(a):
            a.activation(top[:, 11:12], top[:, 10:11], AF.Exp, bias=0.0,
                         scale=1.0).then_inc(H["sa"], 1)
        emit(a_, ftp2)
        v_t2 = inc("sa", 1)
        wait(v_, "sa", v_t2)
        u1 = inc("sv", 1)
        u2 = inc("sv", 1)
        u3 = inc("sv", 1)
        u4 = inc("sv", 1)
        v_t3 = inc("sv", 1)

        def ftp3(v, u1=u1, u2=u2, u3=u3, u4=u4):
            v.tensor_scalar(top[:, 12:13], top[:, 11:12], 1.0, None,
                            ALU.add).then_inc(H["sv"], 1)
            v.wait_ge(H["sv"], u1)
            v.reciprocal(top[:, 13:14], top[:, 12:13]).then_inc(H["sv"], 1)
            v.wait_ge(H["sv"], u2)
            v.tensor_mul(top[:, 14:15], top[:, 11:12],
                         top[:, 13:14]).then_inc(H["sv"], 1)
            v.wait_ge(H["sv"], u3)
            v.tensor_scalar(top[:, 40:48], top[:, 16:24], top[:, 13:14],
                            None, ALU.mult).then_inc(H["sv"], 1)
            v.wait_ge(H["sv"], u4)
            v.scalar_tensor_tensor(top[:, 48:56], top[:, 32:40], top[:, 14:15],
                                   top[:, 40:48], ALU.mult,
                                   ALU.add).then_inc(H["sv"], 1)
        emit(v_, ftp3)
        pe_take_bank(3)
        wait(t_, "sv", v_t3)
        wait(t_, "spre", v_pre4)
        def ftr2(t):
            t.transpose(PS[3][0:8, 0:128], top[:, 48:56],
                        id_sb[:, :]).then_inc(H["st"], 1)
        emit(t_, ftr2)
        v_t4 = inc("st", 1)
        wait(a_, "st", v_t4)
        def fw8(a):
            a.activation(w8t_sb[0:8, :], PS[3][0:8, 0:128], AF.Copy, bias=0.0,
                         scale=1.0).then_inc(H["sa"], 1)
        emit(a_, fw8)
        v_t5 = inc("sa", 1)
        consume_bank(3, "sa", v_t5)
        pe_take_bank(2)
        wait(t_, "sa", v_t5)
        def fwx(t):
            t.matmul(PS[2][:, 0:128], mbr_sb[0:8, :], w8t_sb[0:8, :],
                     start=True, stop=True).then_inc(H["st"], 1)
        emit(t_, fwx)
        v_t6 = inc("st", 1)
        wait(a_, "st", v_t6)
        def fwe(a, tt=tt):
            a.activation(wexp_sb[:, 128 * tt:128 * (tt + 1)], PS[2][:, 0:128],
                         AF.Copy, bias=0.0, scale=1.0).then_inc(H["sa"], 1)
        emit(a_, fwe)
        v_t7 = inc("sa", 1)
        consume_bank(2, "sa", v_t7)
    v_wexp = cur("sa")

    for th in range(2):
        pe_take_bank(0)
        wait(t_, "spre", v_pre4)
        def fh(t, th=th):
            for dc in range(NC):
                mm = t.matmul(
                    PS[0][:, :], a2_sb[:, 128 * dc:128 * (dc + 1)],
                    x2ln[:, 1024 * dc + 512 * th:1024 * dc + 512 * (th + 1)],
                    start=(dc == 0), stop=(dc == NC - 1))
            mm.then_inc(H["st"], 1)
        emit(t_, fh)
        v_h = inc("st", 1)
        wait(v_, "st", v_h)
        wait(v_, "sa", v_wexp)
        def fg(v, th=th):
            v.tensor_mul(g_sb[:, 512 * th:512 * (th + 1)], PS[0][:, :],
                         wexp_sb[:, 512 * th:512 * (th + 1)]).then_inc(H["sv"], 1)
        emit(v_, fg)
        v_g = inc("sv", 1)
        consume_bank(0, "sv", v_g)
    v_g_done = cur("sv")

    fin_pe_done = [0, 0]
    yq_dma = [0, 0]
    v_yadd = [0, 0]
    v_wb = [0, 0]
    v_b2l = [0, 0]
    v_bb = [0, 0]
    v_x2f = [0, 0]
    v_h_pe_done = cur("st")
    for oc in range(32):
        pa = oc % 2
        if oc >= 2:
            wait(g_, "st", fin_pe_done[pa])
        else:
            wait(g_, "st", v_h_pe_done)
        sem_wo = "swo0" if pa == 0 else "swo1"
        if oc >= 2:
            wait(g_, "sv", v_yadd[pa])
        for q4 in range(4):
            v_wb[pa] = gchunk(
                sem_wo, d_wbT[oc, 1024 * q4:1024 * (q4 + 1), :],
                wo_sb[:, 4096 * pa + 1024 * q4:4096 * pa + 1024 * (q4 + 1)],
                128)
        v_b2l[pa] = gdma(sem_wo, d_b2[oc], b2rot[pa])
        v_bb[pa] = gdma(sem_wo, d_bbase[0:1, 128 * oc:128 * (oc + 1)],
                        brot[0:1, 128 * pa:128 * (pa + 1)])
        v_x2f[pa] = ggath("sxl0" if pa == 0 else "sxl1",
                          x2q[pa][:, :], ap_x2, ix2_sb[:, oc:oc + 1])
        wait(t_, sem_wo, v_bb[pa])
        wait(t_, "sv", v_g_done)
        for th in range(2):
            pe_take_bank(2 + th)
            def ffi(t, th=th, pa=pa):
                for dc in range(NC):
                    t.matmul(PS[2 + th][:, :],
                             wo_sb[:, 4096 * pa + 128 * dc:
                                   4096 * pa + 128 * (dc + 1)],
                             x2ln[:, 1024 * dc + 512 * th:
                                  1024 * dc + 512 * (th + 1)],
                             start=(dc == 0), stop=False)
                t.matmul(PS[2 + th][:, :], brot[0:1, 128 * pa:128 * (pa + 1)],
                         ones_b[0:1, 0:512], start=False, stop=False)
                t.matmul(PS[2 + th][:, :], b2rot[pa][:, :],
                         g_sb[:, 512 * th:512 * (th + 1)],
                         start=False, stop=True).then_inc(H["st"], 1)
            emit(t_, ffi)
            inc("st", 1)
        fin_pe_done[pa] = cur("st")
        wait(v_, "st", fin_pe_done[pa])
        wait(v_, "sxl0" if pa == 0 else "sxl1", v_x2f[pa])
        wait(v_, "so0" if pa == 0 else "so1", yq_dma[pa])
        def fyo(v, pa=pa):
            v.tensor_add(x2q[2 + pa][:, 0:512], x2q[pa][:, 0:512],
                         PS[2][:, :])
            v.tensor_add(x2q[2 + pa][:, 512:1024], x2q[pa][:, 512:1024],
                         PS[3][:, :]).then_inc(H["sv"], 1)
        emit(v_, fyo)
        v_y = inc("sv", 1)
        v_yadd[pa] = v_y
        consume_bank(2, "sv", v_y)
        consume_bank(3, "sv", v_y)
        wait(s_, "sv", v_y)
        sem_so = "so0" if pa == 0 else "so1"

        def fys(sy, oc=oc, pa=pa, sem_so=sem_so):
            sy.dma_start(out=d_y[128 * oc:128 * (oc + 1), :],
                         in_=x2q[2 + pa][:, :]).then_inc(H[sem_so], 16)
        emit(s_, fys)
        yq_dma[pa] = inc(sem_so, 16)

    for sem in DMASEMS:
        if cur(sem):
            wait(s_, sem, cur(sem))
    return _assemble(nc, P_, es)


def _assemble(nc, P_, es):
    def _run(eng_name):
        def runner(e):
            for i, f in enumerate(P_.q[eng_name]):
                try:
                    f(e)
                except Exception:
                    import traceback
                    print(f"FAILING CLOSURE {eng_name}[{i}] "
                          f"{getattr(f, '__name__', '?')} defined at line "
                          f"{f.__code__.co_firstlineno}")
                    raise
        return runner

    with nc.Block() as block:
        block.sync(_run("sync"))
        block.gpsimd(_run("gpsimd"))
        block.tensor(_run("tensor"))
        block.vector(_run("vector"))
        block.scalar(_run("scalar"))
    es.close()
    return nc


_NC_CACHE = []


def kernel(**inputs):
    maps = make_host_inputs(inputs)
    if not _NC_CACHE:
        _NC_CACHE.append(build_program())
    nc = _NC_CACHE[0]
    res = run_bass_kernel_spmd(nc, maps, core_ids=list(range(8)))
    out = np.empty((B, S, D), dtype=np.float32)
    for c in range(8):
        b, r = c // 2, c % 2
        out[b, r * T:(r + 1) * T, :] = res.results[c]["y"].T
    return out


# version 06
# speedup vs baseline: 1.5249x; 1.5249x over previous
import dataclasses
import math
import numpy as np
import concourse.bass as bass
import concourse.mybir as mybir
from concourse.bass_utils import run_bass_kernel_spmd

F32 = mybir.dt.float32
BF16 = mybir.dt.bfloat16
I32 = mybir.dt.int32
AF = mybir.ActivationFunctionType
ALU = mybir.AluOpType
AX = mybir.AxisListType

B, S, D = 4, 2048, 4096
NH, HD, NHL = 32, 128, 16
T = 1024
NC = 32
E, RK = 8, 16
SCALING = 32.0 / 16.0
SCALE_QK = float(np.float32(HD) ** np.float32(-0.5))
EPS = 1e-5
MAGIC = 0x5A17C0DE
CEXTRA = 5.0


def _alibi_slopes(n):
    base = 2.0 ** (-(2.0 ** -(math.log2(n) - 3)))
    return np.array([base ** (i + 1) for i in range(n)], dtype=np.float32)


SLOPES = _alibi_slopes(NH)


def _w_for_slope(s):
    for w in (512, 256, 128):
        if s * (w - 1) <= 136.0:
            return w
    return 128


W_HL = [_w_for_slope(float(SLOPES[hl])) for hl in range(NHL)]
NCOL_HL = [16 * (2048 // w) for w in W_HL]
ALI_OFF = np.cumsum([0] + NCOL_HL).tolist()


def _f32(x):
    return np.ascontiguousarray(x, dtype=np.float32)


def make_host_inputs(inp):
    x = np.asarray(inp["L0_attn_hidden_states"], dtype=np.float32)
    w_qkv = np.asarray(inp["L0_attn_w_qkv"], dtype=np.float32)
    b_qkv = np.asarray(inp["L0_attn_b_qkv"], dtype=np.float32)
    w_out = np.asarray(inp["L0_attn_w_out"], dtype=np.float32)
    b_out = np.asarray(inp["L0_attn_b_out"], dtype=np.float32)
    W_base = np.asarray(inp["L0_ffn_W_base"], dtype=np.float32)
    b_base = np.asarray(inp["L0_ffn_b_base"], dtype=np.float32)
    W_router = np.asarray(inp["L0_ffn_W_router"], dtype=np.float32)
    lora_A = np.asarray(inp["L0_ffn_lora_A"], dtype=np.float32)
    lora_B = np.asarray(inp["L0_ffn_lora_B"], dtype=np.float32)

    w_out_r = _f32(w_out.reshape(32, 128, 32, 128)
                   .transpose(2, 1, 0, 3).reshape(32, 128, D))
    wbT_r = _f32(W_base.T.reshape(32, 128, 32, 128)
                 .transpose(2, 1, 0, 3).reshape(32, 128, D))
    wr_r = _f32(W_router.T.reshape(32, 128, E)
                .transpose(1, 0, 2).reshape(128, 32 * E))
    a2_r = _f32(lora_A.reshape(E * RK, D).T.reshape(32, 128, 128)
                .transpose(1, 0, 2).reshape(128, D))
    b2_r = _f32(lora_B.transpose(0, 2, 1).reshape(E * RK, D)
                .reshape(128, 32, 128).transpose(1, 0, 2))

    tri = np.where(np.arange(128)[:, None] > np.arange(128)[None, :],
                   np.float32(-1e9), np.float32(0.0)).astype(np.float32)
    mbr = np.zeros((E, 128), dtype=np.float32)
    for e in range(E):
        mbr[e, e * RK:(e + 1) * RK] = SCALING
    ident = np.eye(128, dtype=np.float32)

    maps = []
    for c in range(8):
        b, r = c // 2, c % 2
        hlo = NHL * r
        xT = _f32(x[b].T)
        xT_own = _f32(xT[:, r * T:(r + 1) * T])
        wq = w_qkv[:, hlo * HD:(hlo + NHL) * HD]
        wk = w_qkv[:, D + hlo * HD:D + (hlo + NHL) * HD]
        wv = w_qkv[:, 2 * D + hlo * HD:2 * D + (hlo + NHL) * HD]
        wqk_t = np.empty((NHL, D, 256), dtype=np.float32)
        for hl in range(NHL):
            wqk_t[hl, :, 0:128] = wq[:, hl * 128:(hl + 1) * 128]
            wqk_t[hl, :, 128:256] = wk[:, hl * 128:(hl + 1) * 128]
        wqk_r = _f32(wqk_t.reshape(NHL, 32, 128, 256)
                     .transpose(0, 2, 1, 3).reshape(NHL, 128, 32 * 256))
        wv_r = _f32(wv.reshape(32, 128, 8, 256)
                    .transpose(2, 1, 0, 3).reshape(8, 128, 32 * 256))
        bq = b_qkv[hlo * HD:(hlo + NHL) * HD]
        bk = b_qkv[D + hlo * HD:D + (hlo + NHL) * HD]
        bv = b_qkv[2 * D + hlo * HD:2 * D + (hlo + NHL) * HD]
        brow = np.empty((8, 768), dtype=np.float32)
        for hp in range(8):
            brow[hp, 0:256] = bq[hp * 256:(hp + 1) * 256]
            brow[hp, 256:512] = bk[hp * 256:(hp + 1) * 256]
            brow[hp, 512:768] = bv[hp * 256:(hp + 1) * 256]
        ali = np.zeros((NHL, 128, 256), dtype=np.float32)
        p_ = np.arange(128, dtype=np.float64)
        for hl in range(NHL):
            sl = float(SLOPES[hlo + hl])
            w = W_HL[hl]
            ng = 2048 // w
            for kt in range(16):
                for g in range(ng):
                    ali[hl, :, kt * ng + g] = (
                        sl * (128 * kt + p_ - w * g - (w - 1) / 2.0) - CEXTRA)
        p32 = np.arange(128, dtype=np.int32)
        ikv = np.empty((128, 48), dtype=np.int32)
        for kind in range(3):
            for hl in range(NHL):
                ikv[:, kind * 16 + hl] = (r * 6144 + kind * 2048
                                          + hl * 128 + p32)
        ictxw = np.empty((128, 32), dtype=np.int32)
        for hl in range(NHL):
            for th in range(2):
                ictxw[:, hl * 2 + th] = (th * 4096 + 2048 * r
                                         + 128 * hl + p32)
        ictxr = np.empty((128, 32), dtype=np.int32)
        ix2 = np.empty((128, 32), dtype=np.int32)
        for gc in range(32):
            ictxr[:, gc] = r * 4096 + 128 * gc + p32
            ix2[:, gc] = r * 4096 + 128 * gc + p32
        maps.append(dict(
            xT=xT, xT_own=xT_own, wqk_r=_f32(wqk_r), wv_r=wv_r,
            brow=_f32(brow), w_out_r=w_out_r,
            b_out_row=_f32(b_out.reshape(1, D)), wbT_r=wbT_r,
            b_base_row=_f32(b_base.reshape(1, D)),
            b2_r=b2_r, alibias=_f32(ali), tri=tri, mbr=mbr, ident=ident,
            wr_r=wr_r, a2_r=a2_r,
            ikv=ikv, ictxw=ictxw, ictxr=ictxr, ix2=ix2,
        ))
    return maps


def dyn(ap, off):
    return dataclasses.replace(ap, offset=off)


class Plan:
    ENGS = ("sync", "gpsimd", "tensor", "vector", "scalar")

    def __init__(self):
        self.q = {e: [] for e in self.ENGS}
        self.cnt = {}
        self.hw = {}
        self.H = {}
        self.R = {}

    def emit(self, eng, fn):
        self.q[eng].append(fn)

    def inc(self, sem, amount=16):
        self.cnt[sem] = self.cnt.get(sem, 0) + amount
        return self.cnt[sem]

    def cur(self, sem):
        return self.cnt.get(sem, 0)

    def wait(self, eng, sem, val):
        if val <= 0:
            return
        assert val <= self.cnt.get(sem, 0), (eng, sem, val, self.cnt.get(sem))
        key = (eng, sem)
        if self.hw.get(key, 0) >= val:
            return
        self.hw[key] = val
        H = self.H

        def f(e, sem=sem, val=val):
            e.wait_ge(H[sem], val)
        self.emit(eng, f)


def build_program():
    import os
    STAGE = int(os.environ.get("KSTAGE", "5"))
    nc = bass.Bass("TRN2", target_bir_lowering=False, num_devices=8)
    P_ = Plan()
    H, R = P_.H, P_.R

    def din(name, shape, dtype=F32):
        return nc.dram_tensor(name, list(shape), dtype, kind="ExternalInput").ap()

    d_xT = din("xT", (D, S))
    d_xTo = din("xT_own", (D, T))
    d_wqk = din("wqk_r", (NHL, 128, 32 * 256))
    d_wv = din("wv_r", (8, 128, 32 * 256))
    d_brow = din("brow", (8, 768))
    d_wout = din("w_out_r", (32, 128, D))
    d_bout = din("b_out_row", (1, D))
    d_wbT = din("wbT_r", (32, 128, D))
    d_bbase = din("b_base_row", (1, D))
    d_wr = din("wr_r", (128, 32 * E))
    d_a2 = din("a2_r", (128, D))
    d_b2 = din("b2_r", (32, 128, 128))
    d_ali = din("alibias", (NHL, 128, 256))
    d_tri = din("tri", (128, 128))
    d_mbr = din("mbr", (E, 128))
    d_id = din("ident", (128, 128))
    d_ikv = din("ikv", (128, 48), I32)
    d_ictxw = din("ictxw", (128, 32), I32)
    d_ictxr = din("ictxr", (128, 32), I32)
    d_ix2 = din("ix2", (128, 32), I32)

    d_y = nc.dram_tensor("y", [D, T], F32, kind="ExternalOutput").ap()

    ap_kv = nc.dram_tensor("sh_kv", [2 * 3 * S, S], BF16, kind="Internal",
                           addr_space="Shared").ap()
    ap_ctx = nc.dram_tensor("sh_ctx", [2 * D, T], BF16, kind="Internal",
                            addr_space="Shared").ap()
    ap_x2 = nc.dram_tensor("sh_x2", [2 * D, T], F32, kind="Internal",
                           addr_space="Shared").ap()
    ap_fl = nc.dram_tensor("sh_fl", [1, 64], I32, kind="Internal",
                           addr_space="Shared").ap()

    from contextlib import ExitStack
    es = ExitStack()

    def sb(name, shape, dtype):
        return es.enter_context(nc.sbuf_tensor(name, list(shape), dtype))

    xln = sb("xln", (128, 65536), BF16)
    wv_sb = sb("wv_sb", (128, 8192), BF16)
    wqk_sb = sb("wqk_sb", (128, 8192), BF16)
    xf32 = sb("xf32", (128, 4096), F32)
    tmp32 = sb("tmp32", (128, 1024), F32)
    bcast = sb("bcast", (128, 2048), F32)
    rows = sb("rows", (1, 1024), F32)
    rows_b = sb("rows_b", (1, 3072), BF16)
    alib = sb("alib", (128, 512), F32)
    trit = sb("trit", (128, 128), F32)
    ones_b = sb("ones_b", (128, 512), BF16)
    brow_sb = sb("brow_sb", (1, 1536), BF16)
    brot = sb("brot", (1, 256), BF16)
    ikv_sb = sb("ikv_sb", (128, 48), I32)
    ictxw_sb = sb("ictxw_sb", (128, 32), I32)
    ictxr_sb = sb("ictxr_sb", (128, 32), I32)
    ix2_sb = sb("ix2_sb", (128, 32), I32)
    cst_eps = sb("cst_eps", (128, 1), F32)
    flag_sb = sb("flag_sb", (1, 16), I32)
    pfl_sb = sb("pfl_sb", (1, 64), I32)
    zero_sb = sb("zero_sb", (1, 16), I32)

    xb16 = xln[:, 0:4096]
    sq16 = xln[:, 4096:6144]
    cx = xln[:, 0:32768]
    att = 32768
    qb = [xln[:, att + 0:att + 2048], xln[:, att + 2048:att + 4096]]
    kb = [xln[:, att + 4096:att + 6144], xln[:, att + 6144:att + 8192]]
    vb = [xln[:, att + 8192:att + 10240], xln[:, att + 10240:att + 12288]]
    pt = [xln[:, att + 12288 + 512 * i:att + 12288 + 512 * (i + 1)]
          for i in range(4)]
    ctxo = [xln[:, att + 14336:att + 15360], xln[:, att + 15360:att + 16384]]
    rbf = xln[:, att + 16384:att + 17408].bitcast(F32)
    tritmp = [xln[:, att + 17408 + 256 * i:att + 17408 + 256 * (i + 1)]
              .bitcast(F32) for i in range(4)]
    x2ln = xln[:, 32768:65536]
    x8 = dataclasses.replace(xf32[:, :], offset=0).bitcast(BF16)
    vtmp = [x8[:, 0:2048], x8[:, 2048:4096]]
    qtmp_bufs = [x8[:, 4096:6144], None]
    ktmp_bufs = [x8[:, 6144:8192], None]
    bc16 = dataclasses.replace(bcast[:, :], offset=0).bitcast(BF16)
    qtmp_bufs[1] = bc16[:, 0:2048]
    ktmp_bufs[1] = bc16[:, 2048:4096]
    wo_sb = wv_sb
    w32 = wqk_sb[:, :].bitcast(F32)
    x2q = [w32[:, 0:1024], w32[:, 1024:2048], w32[:, 2048:3072],
           w32[:, 3072:4096]]
    xb2 = xf32[:, :].bitcast(BF16)
    xb2a = xb2[:, 0:1024]
    xb2b = xb2[:, 1024:2048]
    g_sb = xln[:, 0:1024]
    wexp_sb = xln[:, 1024:2048]
    wr_sb = xln[:, 2048:2304]
    id_sb = xln[:, 2304:2560].bitcast(F32)
    mbr_sb = xln[0:8, 2560:2688]
    top = xln[:, 2688:2816].bitcast(F32)
    w8t_sb = xln[0:8, 2816:2944]
    b2rot = [xln[:, 2944:3072], xln[:, 3072:3200]]
    a2_sb = wv_sb[:, 0:4096]

    PS = [es.enter_context(nc.psum_tensor(f"PS{i}", [128, 512], F32))
          for i in range(8)]

    DMASEMS = ("spre", "sxa", "sxb", "sfa", "sfb", "swv", "swq", "skv",
               "skl0", "skl1", "sct0", "sct1", "sfl", "sga", "sxs0", "sxs1",
               "sxl0", "sxl1", "swo0", "swo1", "sia", "sib", "so0", "so1")
    for s in DMASEMS + ("st", "sv", "sa", "sf", "sq", "sp"):
        P_.H[s] = es.enter_context(nc.semaphore(s))


    nc.const_aps.register(F32, EPS, cst_eps[:, 0:1])

    g_, s_, t_, v_, a_ = "gpsimd", "sync", "tensor", "vector", "scalar"
    inc, cur, wait, emit = P_.inc, P_.cur, P_.wait, P_.emit

    bank_guard = [None] * 8

    def pe_take_bank(i):
        if bank_guard[i] is not None:
            wait(t_, *bank_guard[i])

    def consume_bank(i, sem, val):
        bank_guard[i] = (sem, val)

    def p0_regs(g):
        pid = g.partition_id()
        for nm in ("rank", "flw", "flw1"):
            R[nm] = g.alloc_register(nm)
        g.reg_alu(R["rank"], pid, 1, ALU.bitwise_and)
        g.reg_alu(R["flw"], R["rank"], 16, ALU.mult)
        g.reg_alu(R["flw1"], R["flw"], 8, ALU.add)
        g.memset(ones_b[:, :], 1.0)
        g.memset(cst_eps[:, :], EPS)
        g.memset(flag_sb[:, :], MAGIC)
        g.memset(zero_sb[:, :], 0)
    emit(g_, p0_regs)

    def gdma(sem, src_ap, dst_ap):
        v = inc(sem, 16)

        def f(g, src_ap=src_ap, dst_ap=dst_ap, sem=sem):
            g.dma_start(out=dst_ap, in_=src_ap).then_inc(H[sem], 16)
        emit(g_, f)
        return v

    def gdma_dyn(sem, dst_fn, src_fn):
        v = inc(sem, 16)

        def f(g, sem=sem):
            g.dma_start(out=dst_fn(g), in_=src_fn(g)).then_inc(H[sem], 16)
        emit(g_, f)
        return v

    def sdma(sem, src_ap, dst_ap):
        v = inc(sem, 16)

        def f(sy, src_ap=src_ap, dst_ap=dst_ap, sem=sem):
            sy.dma_start(out=dst_ap, in_=src_ap).then_inc(H[sem], 16)
        emit(s_, f)
        return v

    def gscat(sem, dram_ap_full, idx_sb_col, src_slice):
        v = inc(sem, 16)

        def f(g, dram_ap_full=dram_ap_full, idx_sb_col=idx_sb_col,
              src_slice=src_slice, sem=sem):
            from concourse.bass_core_rust import IndirectOffsetOnAxis
            g.indirect_dma_start(
                dram_ap_full, IndirectOffsetOnAxis(ap=idx_sb_col, axis=0),
                src_slice, None).then_inc(H[sem], 16)
        emit(g_, f)
        return v

    def ggath(sem, dst_slice, dram_ap_full, idx_sb_col):
        v = inc(sem, 16)

        def f(g, dram_ap_full=dram_ap_full, idx_sb_col=idx_sb_col,
              dst_slice=dst_slice, sem=sem):
            from concourse.bass_core_rust import IndirectOffsetOnAxis
            g.indirect_dma_start(
                dst_slice, None,
                dram_ap_full, IndirectOffsetOnAxis(ap=idx_sb_col, axis=0),
            ).then_inc(H[sem], 16)
        emit(g_, f)
        return v

    def gchunk(sem, src2d, dst_slice, width):
        return gdma(sem, src2d.rearrange("(c p) f -> p c f", p=128),
                    dst_slice.rearrange("p (c f) -> p c f", f=width))

    gdma("spre", d_tri, trit[:, :])
    gdma("spre", d_ikv, ikv_sb[:, :])
    gdma("spre", d_ictxw, ictxw_sb[:, :])
    gdma("spre", d_ictxr, ictxr_sb[:, :])
    v_pre = gdma("spre", d_ix2, ix2_sb[:, :])
    emit(g_, lambda g: g.nop().then_inc(H["sq"], 1))
    v_cst_sem = "sq"
    v_cst = inc("sq", 1)

    last_mm_for_parity = [0, 0]
    last_sq_for_parity = [0, 0]
    for dc in range(NC):
        pa = dc % 2
        if dc >= 2:
            wait(g_, "st", last_mm_for_parity[pa])
            wait(g_, "sv", last_sq_for_parity[pa])
        vld = gdma("sxa" if pa == 0 else "sxb",
                   d_xT[128 * dc:128 * (dc + 1), :],
                   xb16[:, 2048 * pa:2048 * (pa + 1)])
        wait(v_, "sxa" if pa == 0 else "sxb", vld)
        def fsq(v, pa=pa):
            v.tensor_mul(sq16[:, :], xb16[:, 2048 * pa:2048 * (pa + 1)],
                         xb16[:, 2048 * pa:2048 * (pa + 1)]).then_inc(H["sv"], 1)
        if dc >= 1:
            wait(v_, "st", last_mm_for_parity[1 - pa])
        emit(v_, fsq)
        vsq = inc("sv", 1)
        last_sq_for_parity[pa] = vsq
        wait(t_, "sxa" if pa == 0 else "sxb", vld)
        wait(t_, "sq", v_cst)
        wait(t_, "sv", vsq)
        def fmm(t, dc=dc, pa=pa):
            for ts in range(4):
                t.matmul(PS[ts][0:1, :], ones_b[:, 0:1],
                         xb16[:, 2048 * pa + 512 * ts:2048 * pa + 512 * (ts + 1)],
                         start=(dc == 0), stop=(dc == NC - 1))
            for ts in range(4):
                mm = t.matmul(PS[4 + ts][0:1, :], ones_b[:, 0:1],
                              sq16[:, 512 * ts:512 * (ts + 1)],
                              start=(dc == 0), stop=(dc == NC - 1))
            mm.then_inc(H["st"], 1)
        emit(t_, fmm)
        last_mm_for_parity[pa] = inc("st", 1)
    v_passA_mm = cur("st")

    def ln_stats_and_bcast(sem_ready_val, s1b, s2b, half_cols, nhalfpieces):
        for i, (b1, b2) in enumerate(zip(s1b, s2b)):
            wait(a_, "st", sem_ready_val)
            va0 = inc("sa", 1)
            va = inc("sa", 1)

            def f1(a, b1=b1, b2=b2, i=i, va0=va0):
                a.activation(rows[0:1, 512 * i:512 * (i + 1)],
                             PS[b1][0:1, :], AF.Copy, bias=0.0,
                             scale=1.0 / D).then_inc(H["sa"], 1)
                a.wait_ge(H["sa"], va0)
                a.activation(PS[b1][0:1, :], PS[b2][0:1, :],
                             AF.Copy, bias=0.0, scale=1.0 / D).then_inc(H["sa"], 1)
            emit(a_, f1)
            wait(v_, "sa", va)
            vv0 = inc("sv", 1)
            vv = inc("sv", 1)

            def f2(v, b1=b1, b2=b2, i=i, vv0=vv0):
                v.tensor_mul(tmp32[0:1, 512 * i:512 * (i + 1)],
                             rows[0:1, 512 * i:512 * (i + 1)],
                             rows[0:1, 512 * i:512 * (i + 1)]).then_inc(H["sv"], 1)
                v.wait_ge(H["sv"], vv0)
                v.scalar_tensor_tensor(PS[b2][0:1, :],
                                       tmp32[0:1, 512 * i:512 * (i + 1)],
                                       -1.0, PS[b1][0:1, :],
                                       ALU.mult, ALU.add).then_inc(H["sv"], 1)
            emit(v_, f2)
            wait(a_, "sv", vv)
            vb0 = inc("sa", 1)
            vb1 = inc("sa", 1)
            va2 = inc("sa", 1)

            def f3(a, b1=b1, b2=b2, i=i, vb0=vb0, vb1=vb1):
                a.activation(PS[b1][0:1, :], PS[b2][0:1, :], AF.Ln,
                             bias=EPS, scale=1.0).then_inc(H["sa"], 1)
                a.wait_ge(H["sa"], vb0)
                a.activation(tmp32[0:1, 512 * i:512 * (i + 1)],
                             PS[b1][0:1, :], AF.Exp, bias=0.0,
                             scale=-0.5).then_inc(H["sa"], 1)
                a.wait_ge(H["sa"], vb1)
                a.activation(rows_b[0:1, 512 * i:512 * (i + 1)],
                             tmp32[0:1, 512 * i:512 * (i + 1)],
                             AF.Copy, bias=0.0, scale=1.0).then_inc(H["sa"], 1)
            emit(a_, f3)
            wait(v_, "sa", va2)
            def f4(v, i=i):
                v.tensor_sub(rows_b[0:1, 1024 + 512 * i:1024 + 512 * (i + 1)],
                             tmp32[0:1, 512 * i:512 * (i + 1)],
                             rows_b[0:1, 512 * i:512 * (i + 1)])
                v.tensor_mul(rows_b[0:1, 2048 + 512 * i:2048 + 512 * (i + 1)],
                             rows[0:1, 512 * i:512 * (i + 1)],
                             tmp32[0:1, 512 * i:512 * (i + 1)]).then_inc(H["sv"], 1)
            emit(v_, f4)
            vlo = inc("sv", 1)
            wait(t_, "sv", vlo)
            def f5(t, b1=b1, b2=b2, i=i):
                t.matmul(PS[b1][:, :], ones_b[0:1, 0:128],
                         rows_b[0:1, 512 * i:512 * (i + 1)], start=True, stop=False)
                t.matmul(PS[b1][:, :], ones_b[0:1, 0:128],
                         rows_b[0:1, 1024 + 512 * i:1024 + 512 * (i + 1)],
                         start=False, stop=True)
                t.matmul(PS[b2][:, :], ones_b[0:1, 0:128],
                         rows_b[0:1, 2048 + 512 * i:2048 + 512 * (i + 1)],
                         start=True, stop=True).then_inc(H["st"], 1)
            emit(t_, f5)
            vmm = inc("st", 1)
            wait(a_, "st", vmm)
            def f6(a, b1=b1, b2=b2, i=i):
                a.activation(bcast[:, 512 * i:512 * (i + 1)], PS[b1][:, :],
                             AF.Copy, bias=0.0, scale=1.0)
                a.activation(bcast[:, 1024 + 512 * i:1024 + 512 * (i + 1)],
                             PS[b2][:, :], AF.Copy, bias=0.0,
                             scale=1.0).then_inc(H["sa"], 1)
            emit(a_, f6)
            v_f6 = inc("sa", 1)
            consume_bank(b1, "sa", v_f6)
            consume_bank(b2, "sa", v_f6)
        return cur("sa")

    xln_written = []
    for h in range(2):
        if h == 1:
            wait(a_, "sv", cur("sv"))
            wait(v_, "sv", cur("sv"))
            wait(t_, "sv", cur("sv"))
        ready = ln_stats_and_bcast(v_passA_mm, [2 * h, 2 * h + 1],
                                   [4 + 2 * h, 4 + 2 * h + 1], None, None)
        lastv = [cur("sv"), cur("sv")]
        for dc in range(NC):
            pa = dc % 2
            wait(s_, "sv", lastv[pa])
            vld = sdma("sfa" if pa == 0 else "sfb",
                       d_xT[128 * dc:128 * (dc + 1), 1024 * h:1024 * (h + 1)],
                       xf32[:, 2048 * pa:2048 * pa + 1024])
            wait(v_, "sfa" if pa == 0 else "sfb", vld)
            wait(v_, "sv", cur("sv"))
            wait(v_, "sa", ready)
            vm0 = inc("sv", 1)
            lastv[pa] = inc("sv", 1)

            def fap(v, dc=dc, pa=pa, h=h, vm0=vm0):
                v.tensor_mul(tmp32[:, :],
                             xf32[:, 2048 * pa:2048 * pa + 1024],
                             bcast[:, 0:1024]).then_inc(H["sv"], 1)
                v.wait_ge(H["sv"], vm0)
                v.tensor_sub(xln[:, 2048 * dc + 1024 * h:2048 * dc + 1024 * (h + 1)],
                             tmp32[:, :], bcast[:, 1024:2048]).then_inc(H["sv"], 1)
            emit(v_, fap)
        xln_written.append(cur("sv"))
    v_ln1_done = cur("sv")

    def dump_and_finish(src_getter, nchunks, chunk_cols, src_f32=False):
        for dc in range(nchunks):
            srcap = src_getter(dc)
            for sem in ("sv", "sa", "st") + DMASEMS:
                wait(v_, sem, cur(sem))
            pa = dc % 2
            def fdp(v, srcap=srcap, pa=pa, chunk_cols=chunk_cols):
                v.tensor_copy(x2q[2 + pa][:, 0:chunk_cols],
                              srcap).then_inc(H["sv"], 1)
            emit(v_, fdp)
            vv = inc("sv", 1)
            wait(s_, "sv", vv)
            def fds(sy, dc=dc, pa=pa, chunk_cols=chunk_cols):
                sy.dma_start(out=d_y[128 * dc:128 * (dc + 1), 0:chunk_cols],
                             in_=x2q[2 + pa][:, 0:chunk_cols]).then_inc(H["so0"], 16)
            emit(s_, fds)
            inc("so0", 16)
        for sem in DMASEMS:
            wait(s_, sem, cur(sem))

    if STAGE == 1:
        dump_and_finish(lambda dc: xln[:, 2048 * dc:2048 * dc + 1024], 32, 1024)
        return _assemble(nc, P_, es)

    v_brow = [0] * 8
    v_wv = [0] * 8
    v_wqk = [0] * NHL
    kv_spilled = []
    pe_wv_done = [0] * 8
    pe_wqk_done = [0] * NHL

    v_pair_spill = [0] * 9
    v_qk_spill = [0] * (NHL + 1)
    for hp in range(8):
        if hp >= 1:
            wait(g_, "st", pe_wv_done[hp - 1])
        if hp >= 2:
            wait(g_, "st", pe_wqk_done[2 * (hp - 2) + 1])
        for q4 in range(4):
            v_wv[hp] = gdma("swv", d_wv[hp, :, 2048 * q4:2048 * (q4 + 1)],
                            wv_sb[:, 2048 * q4:2048 * (q4 + 1)])
        v_brow[hp] = gdma("swv", d_brow[hp:hp + 1, :],
                          brow_sb[0:1, 768 * (hp % 2):768 * (hp % 2) + 768])
        wait(t_, "swv", v_brow[hp])
        wait(t_, "sv", v_ln1_done)
        evac_done = [0] * 8
        VBANKS = (0, 1, 4, 5)
        for tt in range(16):
            bk = VBANKS[tt % 4]
            pe_take_bank(bk)
            def fv(t, tt=tt, bk=bk, hp=hp):
                for dc in range(NC):
                    t.matmul(PS[bk][:, 0:256],
                             xln[:, 2048 * dc + 128 * tt:2048 * dc + 128 * (tt + 1)],
                             wv_sb[:, 256 * dc:256 * (dc + 1)],
                             start=(dc == 0), stop=False)
                t.matmul(PS[bk][:, 0:256], ones_b[0:1, 0:128],
                         brow_sb[0:1, 768 * (hp % 2) + 512:768 * (hp % 2) + 768],
                         start=False, stop=True).then_inc(H["st"], 1)
            emit(t_, fv)
            vmm = inc("st", 1)
            wait(a_, "st", vmm)
            if hp >= 1:
                wait(a_, "skv", cur("skv"))
            def fe(a, tt=tt, bk=bk):
                a.activation(vtmp[0][:, 128 * tt:128 * (tt + 1)],
                             PS[bk][:, 0:128], AF.Copy, bias=0.0, scale=1.0)
                a.activation(vtmp[1][:, 128 * tt:128 * (tt + 1)],
                             PS[bk][:, 128:256], AF.Copy, bias=0.0,
                             scale=1.0).then_inc(H["sa"], 1)
            emit(a_, fe)
            va = inc("sa", 1)
            consume_bank(bk, "sa", va)
            evac_done[bk] = va
        pe_wv_done[hp] = cur("st")
        wait(g_, "sa", max(evac_done))
        wait(g_, "spre", v_pre)
        for ho in range(2):
            hl = 2 * hp + ho
            kv_spilled.append(gscat("skv", ap_kv, ikv_sb[:, 32 + hl:33 + hl],
                                    vtmp[ho][:, :]))
        v_pair_spill[hp] = cur("skv")

        for ho in range(2):
            hl = 2 * hp + ho
            if hl >= 1:
                wait(g_, "st", pe_wqk_done[hl - 1])
            for q4 in range(4):
                v_wqk[hl] = gdma(
                    "swq", d_wqk[hl, :, 2048 * q4:2048 * (q4 + 1)],
                    wqk_sb[:, 2048 * q4:2048 * (q4 + 1)])
            wait(t_, "swq", v_wqk[hl])
            for kind, buf, sc, boff in (
                    (0, qtmp_bufs[hl % 2], SCALE_QK, 128 * ho),
                    (1, ktmp_bufs[hl % 2], 1.0, 256 + 128 * ho)):
                ev = 0
                QKBANKS = (2, 3, 6, 7)
                for ts in range(4):
                    bk = QKBANKS[(4 * kind + ts) % 4]
                    pe_take_bank(bk)
                    def fq(t, ts=ts, bk=bk, kind=kind, hp=hp, boff=boff, ho=ho):
                        for dc in range(NC):
                            t.matmul(PS[bk][:, :],
                                     wqk_sb[:, 256 * dc + 128 * kind:
                                            256 * dc + 128 * (kind + 1)],
                                     xln[:, 2048 * dc + 512 * ts:
                                         2048 * dc + 512 * (ts + 1)],
                                     start=(dc == 0), stop=False)
                        t.matmul(PS[bk][:, :],
                                 brow_sb[0:1, 768 * (hp % 2) + boff:
                                         768 * (hp % 2) + boff + 128],
                                 ones_b[0:1, 0:512],
                                 start=False, stop=True).then_inc(H["st"], 1)
                    emit(t_, fq)
                    vmm = inc("st", 1)
                    wait(a_, "st", vmm)
                    if hl >= 1:
                        wait(a_, "skv", cur("skv"))
                    def fqe(a, ts=ts, bk=bk, buf=buf, sc=sc):
                        a.activation(buf[:, 512 * ts:512 * (ts + 1)], PS[bk][:, :],
                                     AF.Copy, bias=0.0, scale=sc).then_inc(H["sa"], 1)
                    emit(a_, fqe)
                    va = inc("sa", 1)
                    consume_bank(bk, "sa", va)
                    ev = va
                wait(g_, "sa", ev)
                kv_spilled.append(gscat(
                    "skv", ap_kv, ikv_sb[:, 16 * kind + hl:16 * kind + hl + 1],
                    buf[:, :]))
            v_qk_spill[hl] = cur("skv")
            pe_wqk_done[hl] = cur("st")
    v_qkv_pe_done = cur("st")
    v_kv_spill_done = cur("skv")

    if STAGE == 2:
        wait(g_, "skv", v_kv_spill_done)
        for hl in range(NHL):
            ggath("sga", cx[:, 2048 * hl:2048 * (hl + 1)], ap_kv,
                  ikv_sb[:, 16 + hl:17 + hl])
        wait(v_, "sga", cur("sga"))
        dump_and_finish(lambda dc: cx[:, 2048 * dc:2048 * dc + 1024], 16, 1024)
        return _assemble(nc, P_, es)

    def phase_barrier():
        for eng in Plan.ENGS:
            for sem in DMASEMS + ("st", "sv", "sa", "sf"):
                if cur(sem):
                    wait(eng, sem, cur(sem))

    v_wo = [0, 0]
    v_brot = [0, 0]
    v_x2in = [0, 0]

    def plan_oproj_loads(oc):
        pa = oc % 2
        sem_wo = "swo0" if pa == 0 else "swo1"
        for q4 in range(4):
            v_wo[pa] = gdma(
                sem_wo, d_wout[oc, :, 1024 * q4:1024 * (q4 + 1)],
                wo_sb[:, 4096 * pa + 1024 * q4:4096 * pa + 1024 * (q4 + 1)])
        v_brot[pa] = gdma(sem_wo, d_bout[0:1, 128 * oc:128 * (oc + 1)],
                          brot[0:1, 128 * pa:128 * (pa + 1)])
        v_x2in[pa] = sdma("sia" if pa == 0 else "sib",
                          d_xTo[128 * oc:128 * (oc + 1), :], x2q[pa])

    wait(g_, "st", pe_wv_done[7])
    wait(g_, "st", pe_wqk_done[NHL - 1])
    wait(s_, "st", pe_wqk_done[NHL - 1])
    plan_oproj_loads(0)
    plan_oproj_loads(1)

    POLLST = {"regs": None}

    def plan_poll(rnd, sfval):
        def fpoll(sy, rnd=rnd):
            if POLLST["regs"] is None:
                POLLST["regs"] = dict(
                    ne=sy.alloc_register("p_ne"), v0=sy.alloc_register("p_v0"),
                    v1=sy.alloc_register("p_v1"), t=sy.alloc_register("p_t"),
                    po=sy.alloc_register("p_po"))
                Rp = POLLST["regs"]
                sy.reg_alu(Rp["po"], sy.partition_id(), 1, ALU.bitwise_and)
                sy.reg_alu(Rp["po"], Rp["po"], 1, ALU.bitwise_xor)
                sy.reg_alu(Rp["po"], Rp["po"], 16, ALU.mult)
                sy.reg_mov(Rp["t"], 16)
            Rp = POLLST["regs"]
            c0, c1 = (0, 16) if rnd == 0 else (8, 24)
            sy.reg_mov(Rp["ne"], 1)
            with sy.While(Rp["ne"]):
                sy.dma_start(out=pfl_sb[0:1, :],
                             in_=ap_fl).then_inc(H["sp"], 16)
                sy.wait_ge(H["sp"], Rp["t"])
                sy.reg_add(Rp["t"], Rp["t"], 16)
                sy.reg_load(Rp["v0"], pfl_sb[0:1, c0:c0 + 1])
                sy.reg_load(Rp["v1"], pfl_sb[0:1, c1:c1 + 1])
                sy.reg_alu(Rp["v0"], Rp["v0"], MAGIC, ALU.not_equal)
                sy.reg_alu(Rp["v1"], Rp["v1"], MAGIC, ALU.not_equal)
                sy.reg_alu(Rp["ne"], Rp["v0"], Rp["v1"], ALU.bitwise_or)
            if rnd == 1:
                sy.reg_sub(Rp["t"], Rp["t"], 16)
                sy.wait_ge(H["sp"], Rp["t"])
                sy.dma_start(out=dyn(ap_fl[0:1, 0:16], Rp["po"]),
                             in_=zero_sb[0:1, :]).then_inc(H["sfl"], 16)
            sy.nop().then_inc(H["sf"], 1)
        emit(s_, fpoll)
        if rnd == 1:
            inc("sfl", 16)
        assert inc("sf", 1) == sfval

    v_qkvld = [0] * NHL
    pe_head_done = [0] * NHL
    last_scat = 0

    v_alild = [0] * NHL

    def plan_kv_load(hl):
        pa = hl % 2
        sem = "skl0" if pa == 0 else "skl1"
        if hl >= 2:
            wait(g_, "st", pe_head_done[hl - 2])
            wait(g_, "sa", cur("sa"))
        wait(g_, "skv", v_kv_spill_done)
        ncol = NCOL_HL[hl]
        v_alild[hl] = gdma(sem, d_ali[hl, :, 0:ncol],
                           alib[:, 256 * pa:256 * pa + ncol])
        for kind, buf in ((0, qb[pa]), (1, kb[pa]), (2, vb[pa])):
            ggath(sem, buf[:, :], ap_kv,
                  ikv_sb[:, 16 * kind + hl:16 * kind + hl + 1])
        return cur(sem)

    v_qkvld[0] = plan_kv_load(0)
    v_qkvld[1] = plan_kv_load(1)

    pt_consumed = [0, 0, 0, 0]
    ctxo_dma = [0, 0]
    exp_slot_war = {}

    for hl in range(NHL):
        pa = hl % 2
        w = W_HL[hl]
        ng = 2048 // w
        for j in range(4):
            kts = list(range(4 * j + 4))
            blocks = {}
            v_ex_slot = dict(exp_slot_war)

            def plan_sc(kt):
                diag = kt >= 4 * j
                m = kt - 4 * j if diag else 0
                npr = 512 - 128 * m
                c0g = 512 * j + 128 * m
                bk = kt % 4
                pe_take_bank(bk)
                wait(t_, "skl0" if pa == 0 else "skl1", v_qkvld[hl])

                def fsc(t, bk=bk, kt=kt, npr=npr, c0g=c0g, j=j, pa=pa):
                    t.matmul(PS[bk][:, 0:npr],
                             kb[pa][:, 128 * kt:128 * (kt + 1)],
                             qb[pa][:, c0g:512 * (j + 1)],
                             start=True, stop=True).then_inc(H["st"], 1)
                emit(t_, fsc)
                v_sc = inc("st", 1)
                v_tri_v = 0
                if diag:
                    wait(v_, "st", v_sc)
                    wait(v_, "spre", v_pre)
                    if v_ex_slot.get(("tri", bk), 0):
                        wait(v_, "sa", v_ex_slot[("tri", bk)])

                    def ftr(v, bk=bk):
                        v.tensor_add(tritmp[bk][:, 0:128], PS[bk][:, 0:128],
                                     trit[:, :]).then_inc(H["sv"], 1)
                    emit(v_, ftr)
                    v_tri_v = inc("sv", 1)
                wait(a_, "st", max(v_sc, pt_consumed[bk]))
                if diag:
                    wait(a_, "sv", v_tri_v)
                wait(a_, "skl0" if pa == 0 else "skl1", v_qkvld[hl])
                calls = []
                a0 = c0g
                while a0 < 512 * (j + 1):
                    gg = a0 // w
                    b0 = min((gg + 1) * w, 512 * (j + 1))
                    if diag and a0 == c0g:
                        b0 = min(b0, c0g + 128)
                    calls.append((a0, b0, gg))
                    a0 = b0

                def fex(a, calls=calls, bk=bk, diag=diag, c0g=c0g, hl=hl,
                        kt=kt, ng=ng, pa=pa):
                    for (a0, b0, gg) in calls:
                        colb = 256 * pa + kt * ng + gg
                        if diag and a0 == c0g:
                            src_ = tritmp[bk][:, 0:b0 - a0]
                        else:
                            src_ = PS[bk][:, a0 - c0g:b0 - c0g]
                        ai_ = a.activation(pt[bk][:, a0 - c0g:b0 - c0g], src_,
                                           AF.Exp,
                                           bias=alib[:, colb:colb + 1],
                                           scale=1.0)
                    ai_.then_inc(H["sa"], 1)
                emit(a_, fex)
                v_ex = inc("sa", 1)
                consume_bank(bk, "sa", v_ex)
                v_ex_slot[("tri", bk)] = v_ex
                exp_slot_war[("tri", bk)] = v_ex
                blocks[kt] = (v_ex, npr, c0g)

            def plan_cd(kt):
                v_ex, npr, c0g = blocks[kt]
                bk = kt % 4
                if kt == 0:
                    pe_take_bank(4)
                    pe_take_bank(5)
                wait(t_, "sa", v_ex)
                last = (kt == kts[-1])

                def fcd(t, bk=bk, kt=kt, npr=npr, c0g=c0g, j=j, pa=pa,
                        last=last):
                    t.matmul(PS[4][:, c0g - 512 * j:512],
                             vb[pa][:, 128 * kt:128 * (kt + 1)],
                             pt[bk][:, 0:npr],
                             start=(kt == 0), stop=last)
                    t.matmul(PS[5][0:1, c0g - 512 * j:512],
                             ones_b[:, 0:1], pt[bk][:, 0:npr],
                             start=(kt == 0), stop=last).then_inc(H["st"], 1)
                emit(t_, fcd)
                pt_consumed[bk] = inc("st", 1)

            DEPTH = 3
            for i, kt in enumerate(kts):
                plan_sc(kt)
                if i >= DEPTH:
                    plan_cd(kts[i - DEPTH])
            for kt in kts[-DEPTH:] if len(kts) >= DEPTH else kts:
                plan_cd(kt)
            v_cd = cur("st")
            wait(v_, "st", v_cd)
            def fr1(v):
                v.reciprocal(rows[0:1, 0:512], PS[5][0:1, :]).then_inc(H["sv"], 1)
            emit(v_, fr1)
            v_rec = inc("sv", 1)
            consume_bank(5, "sv", v_rec)
            wait(a_, "sv", v_rec)
            def fr2(a):
                a.activation(rows_b[0:1, 0:512], rows[0:1, 0:512], AF.Copy,
                             bias=0.0, scale=1.0).then_inc(H["sa"], 1)
            emit(a_, fr2)
            v_hi = inc("sa", 1)
            wait(v_, "sa", v_hi)
            def fr3(v):
                v.tensor_sub(rows_b[0:1, 512:1024], rows[0:1, 0:512],
                             rows_b[0:1, 0:512]).then_inc(H["sv"], 1)
            emit(v_, fr3)
            v_lo = inc("sv", 1)
            pe_take_bank(6)
            wait(t_, "sv", v_lo)
            def fr4(t):
                t.matmul(PS[6][:, :], ones_b[0:1, 0:128], rows_b[0:1, 0:512],
                         start=True, stop=False)
                t.matmul(PS[6][:, :], ones_b[0:1, 0:128], rows_b[0:1, 512:1024],
                         start=False, stop=True).then_inc(H["st"], 1)
            emit(t_, fr4)
            v_bc = inc("st", 1)
            wait(a_, "st", v_bc)
            def fr5(a):
                a.activation(rbf[:, :], PS[6][:, :], AF.Copy, bias=0.0,
                             scale=1.0).then_inc(H["sa"], 1)
            emit(a_, fr5)
            v_rb = inc("sa", 1)
            consume_bank(6, "sa", v_rb)
            thalf = j // 2
            jh = j % 2
            wait(v_, "sa", v_rb)
            wait(v_, "sct0" if thalf == 0 else "sct1", ctxo_dma[thalf])
            def fr6(v, thalf=thalf, jh=jh):
                v.tensor_mul(ctxo[thalf][:, 512 * jh:512 * (jh + 1)],
                             PS[4][:, :], rbf[:, :]).then_inc(H["sv"], 1)
            emit(v_, fr6)
            v_cm = inc("sv", 1)
            consume_bank(4, "sv", v_cm)
            if jh == 1:
                wait(g_, "sv", v_cm)
                ctxo_dma[thalf] = gscat(
                    "sct0" if thalf == 0 else "sct1", ap_ctx,
                    ictxw_sb[:, hl * 2 + thalf:hl * 2 + thalf + 1],
                    ctxo[thalf][:, :])
        pe_head_done[hl] = cur("st")
        if hl + 2 < NHL:
            v_qkvld[hl + 2] = plan_kv_load(hl + 2)
        if hl == 7:
            wait(g_, "sct0", cur("sct0"))
            wait(g_, "sct1", cur("sct1"))
            gdma_dyn("sfl", lambda g: dyn(ap_fl[0:1, 0:8], R["flw"]),
                     lambda g: flag_sb[0:1, 0:8])
            plan_poll(0, 1)
            wait(g_, "sf", 1)
            for gc in list(range(0, 8)) + list(range(16, 24)):
                ggath("sga", cx[:, 1024 * gc:1024 * (gc + 1)], ap_ctx,
                      ictxr_sb[:, gc:gc + 1])

    wait(g_, "sct0", cur("sct0"))
    wait(g_, "sct1", cur("sct1"))
    v_flag = gdma_dyn(
        "sfl",
        lambda g: dyn(ap_fl[0:1, 8:16], R["flw1"]),
        lambda g: flag_sb[0:1, 0:8])

    plan_poll(1, 2)
    v_poll = cur("sf")

    wait(g_, "sf", v_poll)
    for gc in list(range(8, 16)) + list(range(24, 32)):
        ggath("sga", cx[:, 1024 * gc:1024 * (gc + 1)], ap_ctx,
              ictxr_sb[:, gc:gc + 1])
    v_gather = cur("sga")

    if STAGE == 3:
        wait(v_, "sga", v_gather)
        dump_and_finish(lambda dc: cx[:, 1024 * dc:1024 * (dc + 1)], 32, 1024)
        return _assemble(nc, P_, es)

    phase_barrier()
    wo_pe_done = [0, 0]
    x2out_dma = [0, 0]
    v_x2add = [0, 0]
    for oc in range(32):
        pa = oc % 2
        sem_wo = "swo0" if pa == 0 else "swo1"
        if oc >= 2:
            wait(g_, "st", wo_pe_done[pa])
            wait(s_, "sv", v_x2add[pa])
            plan_oproj_loads(oc)
        wait(t_, sem_wo, v_brot[pa])
        wait(t_, "sga", v_gather)
        for th in range(2):
            pe_take_bank(th)
            def fop(t, th=th, pa=pa):
                for fc in range(32):
                    t.matmul(PS[th][:, :],
                             wo_sb[:, 4096 * pa + 128 * fc:4096 * pa + 128 * (fc + 1)],
                             cx[:, 1024 * fc + 512 * th:1024 * fc + 512 * (th + 1)],
                             start=(fc == 0), stop=False)
                t.matmul(PS[th][:, :], brot[0:1, 128 * pa:128 * (pa + 1)],
                         ones_b[0:1, 0:512],
                         start=False, stop=True).then_inc(H["st"], 1)
            emit(t_, fop)
            inc("st", 1)
        wo_pe_done[pa] = cur("st")
        wait(v_, "sia" if pa == 0 else "sib", v_x2in[pa])
        wait(v_, "st", wo_pe_done[pa])
        wait(v_, "sxs0" if pa == 0 else "sxs1", x2out_dma[pa])
        def fx2(v, pa=pa):
            v.tensor_add(x2q[2 + pa][:, 0:512], x2q[pa][:, 0:512], PS[0][:, :])
            v.tensor_add(x2q[2 + pa][:, 512:1024], x2q[pa][:, 512:1024],
                         PS[1][:, :]).then_inc(H["sv"], 1)
        emit(v_, fx2)
        v_add = inc("sv", 1)
        v_x2add[pa] = v_add
        consume_bank(0, "sv", v_add)
        consume_bank(1, "sv", v_add)
        wait(a_, "sv", v_add)
        def fcb(a, pa=pa):
            a.activation(xb2a[:, :], x2q[2 + pa][:, :], AF.Copy, bias=0.0,
                         scale=1.0).then_inc(H["sa"], 1)
        emit(a_, fcb)
        v_c1 = inc("sa", 1)
        wait(v_, "sa", v_c1)
        def fsq2(v):
            v.tensor_mul(xb2b[:, :], xb2a[:, :], xb2a[:, :]).then_inc(H["sv"], 1)
        emit(v_, fsq2)
        v_s2 = inc("sv", 1)
        if oc == 0:
            pe_take_bank(4), pe_take_bank(5), pe_take_bank(6), pe_take_bank(7)
        wait(t_, "sv", v_s2)
        def fst(t, oc=oc):
            for th2 in range(2):
                t.matmul(PS[4 + th2][0:1, :], ones_b[:, 0:1],
                         xb2a[:, 512 * th2:512 * (th2 + 1)],
                         start=(oc == 0), stop=(oc == 31))
                mm = t.matmul(PS[6 + th2][0:1, :], ones_b[:, 0:1],
                              xb2b[:, 512 * th2:512 * (th2 + 1)],
                              start=(oc == 0), stop=(oc == 31))
            mm.then_inc(H["st"], 1)
        emit(t_, fst)
        v_stm = inc("st", 1)
        wait(g_, "sv", v_add)
        x2out_dma[pa] = gscat("sxs0" if pa == 0 else "sxs1",
                              ap_x2, ix2_sb[:, oc:oc + 1],
                              x2q[2 + pa][:, :])
    v_x2s0, v_x2s1 = cur("sxs0"), cur("sxs1")
    v_ln2_mm = cur("st")

    wait(a_, "sv", cur("sv"))
    ln_stats_and_bcast(v_ln2_mm, [4, 5], [6, 7], None, None)
    v_ln2_ready = cur("sa")

    x2ln_parts = [0, 0]
    for dc in range(32):
        pa = dc % 2
        wait(g_, "sxs0", v_x2s0)
        wait(g_, "sxs1", v_x2s1)
        if dc >= 2:
            wait(g_, "sv", x2ln_parts[pa])
        vld = ggath("sxl0" if pa == 0 else "sxl1",
                    x2q[pa][:, :], ap_x2, ix2_sb[:, dc:dc + 1])
        wait(v_, "sxl0" if pa == 0 else "sxl1", vld)
        wait(v_, "sv", cur("sv"))
        wait(v_, "sa", v_ln2_ready)
        vm0 = inc("sv", 1)
        x2ln_parts[pa] = inc("sv", 1)

        def fl2(v, dc=dc, pa=pa, vm0=vm0):
            v.tensor_mul(tmp32[:, :], x2q[pa][:, :],
                         bcast[:, 0:1024]).then_inc(H["sv"], 1)
            v.wait_ge(H["sv"], vm0)
            v.tensor_sub(x2ln[:, 1024 * dc:1024 * (dc + 1)], tmp32[:, :],
                         bcast[:, 1024:2048]).then_inc(H["sv"], 1)
        emit(v_, fl2)
    v_x2ln_done = cur("sv")

    if STAGE == 4:
        wait(v_, "sv", v_x2ln_done)
        dump_and_finish(lambda dc: x2ln[:, 1024 * dc:1024 * (dc + 1)], 32, 1024)
        return _assemble(nc, P_, es)

    phase_barrier()
    gdma("spre", d_wr, wr_sb)
    gdma("spre", d_id, id_sb)
    gdma("spre", d_mbr, mbr_sb)
    v_pre4 = gdma("spre", d_a2, a2_sb)

    for tt in range(8):
        pe_take_bank(2)
        wait(t_, "spre", v_pre4)
        wait(t_, "sv", v_x2ln_done)
        def flg(t, tt=tt):
            for dc in range(NC):
                mm = t.matmul(
                    PS[2][:, 0:8],
                    x2ln[:, 1024 * dc + 128 * tt:1024 * dc + 128 * (tt + 1)],
                    wr_sb[:, 8 * dc:8 * (dc + 1)],
                    start=(dc == 0), stop=(dc == NC - 1))
            mm.then_inc(H["st"], 1)
        emit(t_, flg)
        v_lg = inc("st", 1)
        wait(a_, "st", v_lg)
        def fle(a):
            a.activation(top[:, 0:8], PS[2][:, 0:8], AF.Copy, bias=0.0,
                         scale=1.0).then_inc(H["sa"], 1)
        emit(a_, fle)
        v_le = inc("sa", 1)
        consume_bank(2, "sa", v_le)
        wait(v_, "sa", v_le)
        w1 = inc("sv", 1)
        w2 = inc("sv", 1)
        w3 = inc("sv", 1)
        w4 = inc("sv", 1)
        w5 = inc("sv", 1)
        v_t1 = inc("sv", 1)

        def ftp1(v, w1=w1, w2=w2, w3=w3, w4=w4, w5=w5):
            v.reduce_max(top[:, 8:9], top[:, 0:8], AX.X).then_inc(H["sv"], 1)
            v.wait_ge(H["sv"], w1)
            v.tensor_scalar(top[:, 16:24], top[:, 0:8], top[:, 8:9],
                            None, ALU.is_ge).then_inc(H["sv"], 1)
            v.wait_ge(H["sv"], w2)
            v.scalar_tensor_tensor(top[:, 24:32], top[:, 16:24], -1e30,
                                   top[:, 0:8], ALU.mult,
                                   ALU.add).then_inc(H["sv"], 1)
            v.wait_ge(H["sv"], w3)
            v.reduce_max(top[:, 9:10], top[:, 24:32],
                         AX.X).then_inc(H["sv"], 1)
            v.wait_ge(H["sv"], w4)
            v.tensor_scalar(top[:, 32:40], top[:, 24:32], top[:, 9:10], None,
                            ALU.is_ge).then_inc(H["sv"], 1)
            v.wait_ge(H["sv"], w5)
            v.tensor_sub(top[:, 10:11], top[:, 9:10],
                         top[:, 8:9]).then_inc(H["sv"], 1)
        emit(v_, ftp1)
        wait(a_, "sv", v_t1)
        def ftp2(a):
            a.activation(top[:, 11:12], top[:, 10:11], AF.Exp, bias=0.0,
                         scale=1.0).then_inc(H["sa"], 1)
        emit(a_, ftp2)
        v_t2 = inc("sa", 1)
        wait(v_, "sa", v_t2)
        u1 = inc("sv", 1)
        u2 = inc("sv", 1)
        u3 = inc("sv", 1)
        u4 = inc("sv", 1)
        v_t3 = inc("sv", 1)

        def ftp3(v, u1=u1, u2=u2, u3=u3, u4=u4):
            v.tensor_scalar(top[:, 12:13], top[:, 11:12], 1.0, None,
                            ALU.add).then_inc(H["sv"], 1)
            v.wait_ge(H["sv"], u1)
            v.reciprocal(top[:, 13:14], top[:, 12:13]).then_inc(H["sv"], 1)
            v.wait_ge(H["sv"], u2)
            v.tensor_mul(top[:, 14:15], top[:, 11:12],
                         top[:, 13:14]).then_inc(H["sv"], 1)
            v.wait_ge(H["sv"], u3)
            v.tensor_scalar(top[:, 40:48], top[:, 16:24], top[:, 13:14],
                            None, ALU.mult).then_inc(H["sv"], 1)
            v.wait_ge(H["sv"], u4)
            v.scalar_tensor_tensor(top[:, 48:56], top[:, 32:40], top[:, 14:15],
                                   top[:, 40:48], ALU.mult,
                                   ALU.add).then_inc(H["sv"], 1)
        emit(v_, ftp3)
        pe_take_bank(3)
        wait(t_, "sv", v_t3)
        wait(t_, "spre", v_pre4)
        def ftr2(t):
            t.transpose(PS[3][0:8, 0:128], top[:, 48:56],
                        id_sb[:, :]).then_inc(H["st"], 1)
        emit(t_, ftr2)
        v_t4 = inc("st", 1)
        wait(a_, "st", v_t4)
        def fw8(a):
            a.activation(w8t_sb[0:8, :], PS[3][0:8, 0:128], AF.Copy, bias=0.0,
                         scale=1.0).then_inc(H["sa"], 1)
        emit(a_, fw8)
        v_t5 = inc("sa", 1)
        consume_bank(3, "sa", v_t5)
        pe_take_bank(2)
        wait(t_, "sa", v_t5)
        def fwx(t):
            t.matmul(PS[2][:, 0:128], mbr_sb[0:8, :], w8t_sb[0:8, :],
                     start=True, stop=True).then_inc(H["st"], 1)
        emit(t_, fwx)
        v_t6 = inc("st", 1)
        wait(a_, "st", v_t6)
        def fwe(a, tt=tt):
            a.activation(wexp_sb[:, 128 * tt:128 * (tt + 1)], PS[2][:, 0:128],
                         AF.Copy, bias=0.0, scale=1.0).then_inc(H["sa"], 1)
        emit(a_, fwe)
        v_t7 = inc("sa", 1)
        consume_bank(2, "sa", v_t7)
    v_wexp = cur("sa")

    for th in range(2):
        pe_take_bank(0)
        wait(t_, "spre", v_pre4)
        def fh(t, th=th):
            for dc in range(NC):
                mm = t.matmul(
                    PS[0][:, :], a2_sb[:, 128 * dc:128 * (dc + 1)],
                    x2ln[:, 1024 * dc + 512 * th:1024 * dc + 512 * (th + 1)],
                    start=(dc == 0), stop=(dc == NC - 1))
            mm.then_inc(H["st"], 1)
        emit(t_, fh)
        v_h = inc("st", 1)
        wait(v_, "st", v_h)
        wait(v_, "sa", v_wexp)
        def fg(v, th=th):
            v.tensor_mul(g_sb[:, 512 * th:512 * (th + 1)], PS[0][:, :],
                         wexp_sb[:, 512 * th:512 * (th + 1)]).then_inc(H["sv"], 1)
        emit(v_, fg)
        v_g = inc("sv", 1)
        consume_bank(0, "sv", v_g)
    v_g_done = cur("sv")

    fin_pe_done = [0, 0]
    yq_dma = [0, 0]
    v_yadd = [0, 0]
    v_wb = [0, 0]
    v_b2l = [0, 0]
    v_bb = [0, 0]
    v_x2f = [0, 0]
    v_h_pe_done = cur("st")
    for oc in range(32):
        pa = oc % 2
        if oc >= 2:
            wait(g_, "st", fin_pe_done[pa])
        else:
            wait(g_, "st", v_h_pe_done)
        sem_wo = "swo0" if pa == 0 else "swo1"
        if oc >= 2:
            wait(g_, "sv", v_yadd[pa])
        for q4 in range(4):
            v_wb[pa] = gdma(
                sem_wo, d_wbT[oc, :, 1024 * q4:1024 * (q4 + 1)],
                wo_sb[:, 4096 * pa + 1024 * q4:4096 * pa + 1024 * (q4 + 1)])
        v_b2l[pa] = gdma(sem_wo, d_b2[oc], b2rot[pa])
        v_bb[pa] = gdma(sem_wo, d_bbase[0:1, 128 * oc:128 * (oc + 1)],
                        brot[0:1, 128 * pa:128 * (pa + 1)])
        v_x2f[pa] = ggath("sxl0" if pa == 0 else "sxl1",
                          x2q[pa][:, :], ap_x2, ix2_sb[:, oc:oc + 1])
        wait(t_, sem_wo, v_bb[pa])
        wait(t_, "sv", v_g_done)
        for th in range(2):
            pe_take_bank(2 + th)
            def ffi(t, th=th, pa=pa):
                for dc in range(NC):
                    t.matmul(PS[2 + th][:, :],
                             wo_sb[:, 4096 * pa + 128 * dc:
                                   4096 * pa + 128 * (dc + 1)],
                             x2ln[:, 1024 * dc + 512 * th:
                                  1024 * dc + 512 * (th + 1)],
                             start=(dc == 0), stop=False)
                t.matmul(PS[2 + th][:, :], brot[0:1, 128 * pa:128 * (pa + 1)],
                         ones_b[0:1, 0:512], start=False, stop=False)
                t.matmul(PS[2 + th][:, :], b2rot[pa][:, :],
                         g_sb[:, 512 * th:512 * (th + 1)],
                         start=False, stop=True).then_inc(H["st"], 1)
            emit(t_, ffi)
            inc("st", 1)
        fin_pe_done[pa] = cur("st")
        wait(v_, "st", fin_pe_done[pa])
        wait(v_, "sxl0" if pa == 0 else "sxl1", v_x2f[pa])
        wait(v_, "so0" if pa == 0 else "so1", yq_dma[pa])
        def fyo(v, pa=pa):
            v.tensor_add(x2q[2 + pa][:, 0:512], x2q[pa][:, 0:512],
                         PS[2][:, :])
            v.tensor_add(x2q[2 + pa][:, 512:1024], x2q[pa][:, 512:1024],
                         PS[3][:, :]).then_inc(H["sv"], 1)
        emit(v_, fyo)
        v_y = inc("sv", 1)
        v_yadd[pa] = v_y
        consume_bank(2, "sv", v_y)
        consume_bank(3, "sv", v_y)
        wait(s_, "sv", v_y)
        sem_so = "so0" if pa == 0 else "so1"

        def fys(sy, oc=oc, pa=pa, sem_so=sem_so):
            sy.dma_start(out=d_y[128 * oc:128 * (oc + 1), :],
                         in_=x2q[2 + pa][:, :]).then_inc(H[sem_so], 16)
        emit(s_, fys)
        yq_dma[pa] = inc(sem_so, 16)

    for sem in DMASEMS:
        if cur(sem):
            wait(s_, sem, cur(sem))
    return _assemble(nc, P_, es)


def _assemble(nc, P_, es):
    def _run(eng_name):
        def runner(e):
            for i, f in enumerate(P_.q[eng_name]):
                try:
                    f(e)
                except Exception:
                    import traceback
                    print(f"FAILING CLOSURE {eng_name}[{i}] "
                          f"{getattr(f, '__name__', '?')} defined at line "
                          f"{f.__code__.co_firstlineno}")
                    raise
        return runner

    with nc.Block() as block:
        block.sync(_run("sync"))
        block.gpsimd(_run("gpsimd"))
        block.tensor(_run("tensor"))
        block.vector(_run("vector"))
        block.scalar(_run("scalar"))
    es.close()
    return nc


_NC_CACHE = []


def kernel(**inputs):
    maps = make_host_inputs(inputs)
    if not _NC_CACHE:
        _NC_CACHE.append(build_program())
    nc = _NC_CACHE[0]
    res = run_bass_kernel_spmd(nc, maps, core_ids=list(range(8)))
    out = np.empty((B, S, D), dtype=np.float32)
    for c in range(8):
        b, r = c // 2, c % 2
        out[b, r * T:(r + 1) * T, :] = res.results[c]["y"].T
    return out


# version 07 (final)
# speedup vs baseline: 1.5578x; 1.0216x over previous
import dataclasses
import math
import numpy as np
import concourse.bass as bass
import concourse.mybir as mybir
from concourse.bass_utils import run_bass_kernel_spmd

F32 = mybir.dt.float32
BF16 = mybir.dt.bfloat16
I32 = mybir.dt.int32
AF = mybir.ActivationFunctionType
ALU = mybir.AluOpType
AX = mybir.AxisListType

B, S, D = 4, 2048, 4096
NH, HD, NHL = 32, 128, 16
T = 1024
NC = 32
E, RK = 8, 16
SCALING = 32.0 / 16.0
SCALE_QK = float(np.float32(HD) ** np.float32(-0.5))
EPS = 1e-5
MAGIC = 0x5A17C0DE
CEXTRA = 5.0


def _alibi_slopes(n):
    base = 2.0 ** (-(2.0 ** -(math.log2(n) - 3)))
    return np.array([base ** (i + 1) for i in range(n)], dtype=np.float32)


SLOPES = _alibi_slopes(NH)


def _w_for_slope(s):
    for w in (512, 256, 128):
        if s * (w - 1) <= 136.0:
            return w
    return 128


W_HL = [_w_for_slope(float(SLOPES[hl])) for hl in range(NHL)]
NCOL_HL = [16 * (2048 // w) for w in W_HL]
ALI_OFF = np.cumsum([0] + NCOL_HL).tolist()


def _f32(x):
    return np.ascontiguousarray(x, dtype=np.float32)


def make_host_inputs(inp):
    x = np.asarray(inp["L0_attn_hidden_states"], dtype=np.float32)
    w_qkv = np.asarray(inp["L0_attn_w_qkv"], dtype=np.float32)
    b_qkv = np.asarray(inp["L0_attn_b_qkv"], dtype=np.float32)
    w_out = np.asarray(inp["L0_attn_w_out"], dtype=np.float32)
    b_out = np.asarray(inp["L0_attn_b_out"], dtype=np.float32)
    W_base = np.asarray(inp["L0_ffn_W_base"], dtype=np.float32)
    b_base = np.asarray(inp["L0_ffn_b_base"], dtype=np.float32)
    W_router = np.asarray(inp["L0_ffn_W_router"], dtype=np.float32)
    lora_A = np.asarray(inp["L0_ffn_lora_A"], dtype=np.float32)
    lora_B = np.asarray(inp["L0_ffn_lora_B"], dtype=np.float32)

    w_out_r = _f32(w_out.reshape(32, 128, 32, 128)
                   .transpose(2, 1, 0, 3).reshape(32, 128, D))
    wbT_r = _f32(W_base.T.reshape(32, 128, 32, 128)
                 .transpose(2, 1, 0, 3).reshape(32, 128, D))
    wr_r = _f32(W_router.T.reshape(32, 128, E)
                .transpose(1, 0, 2).reshape(128, 32 * E))
    a2_r = _f32(lora_A.reshape(E * RK, D).T.reshape(32, 128, 128)
                .transpose(1, 0, 2).reshape(128, D))
    b2_r = _f32(lora_B.transpose(0, 2, 1).reshape(E * RK, D)
                .reshape(128, 32, 128).transpose(1, 0, 2))

    tri = np.where(np.arange(128)[:, None] > np.arange(128)[None, :],
                   np.float32(-1e9), np.float32(0.0)).astype(np.float32)
    mbr = np.zeros((E, 128), dtype=np.float32)
    for e in range(E):
        mbr[e, e * RK:(e + 1) * RK] = SCALING
    ident = np.eye(128, dtype=np.float32)

    maps = []
    for c in range(8):
        b, r = c // 2, c % 2
        hlo = NHL * r
        xT = _f32(x[b].T)
        xT_own = _f32(xT[:, r * T:(r + 1) * T])
        wq = w_qkv[:, hlo * HD:(hlo + NHL) * HD]
        wk = w_qkv[:, D + hlo * HD:D + (hlo + NHL) * HD]
        wv = w_qkv[:, 2 * D + hlo * HD:2 * D + (hlo + NHL) * HD]
        wqk_t = np.empty((NHL, D, 256), dtype=np.float32)
        for hl in range(NHL):
            wqk_t[hl, :, 0:128] = wq[:, hl * 128:(hl + 1) * 128]
            wqk_t[hl, :, 128:256] = wk[:, hl * 128:(hl + 1) * 128]
        wqk_r = _f32(wqk_t.reshape(NHL, 32, 128, 256)
                     .transpose(0, 2, 1, 3).reshape(NHL, 128, 32 * 256))
        wv_r = _f32(wv.reshape(32, 128, 8, 256)
                    .transpose(2, 1, 0, 3).reshape(8, 128, 32 * 256))
        bq = b_qkv[hlo * HD:(hlo + NHL) * HD]
        bk = b_qkv[D + hlo * HD:D + (hlo + NHL) * HD]
        bv = b_qkv[2 * D + hlo * HD:2 * D + (hlo + NHL) * HD]
        brow = np.empty((8, 768), dtype=np.float32)
        for hp in range(8):
            brow[hp, 0:256] = bq[hp * 256:(hp + 1) * 256]
            brow[hp, 256:512] = bk[hp * 256:(hp + 1) * 256]
            brow[hp, 512:768] = bv[hp * 256:(hp + 1) * 256]
        ali = np.zeros((NHL, 128, 256), dtype=np.float32)
        p_ = np.arange(128, dtype=np.float64)
        for hl in range(NHL):
            sl = float(SLOPES[hlo + hl])
            w = W_HL[hl]
            ng = 2048 // w
            for kt in range(16):
                for g in range(ng):
                    ali[hl, :, kt * ng + g] = (
                        sl * (128 * kt + p_ - w * g - (w - 1) / 2.0) - CEXTRA)
        p32 = np.arange(128, dtype=np.int32)
        ikv = np.empty((128, 48), dtype=np.int32)
        for kind in range(3):
            for hl in range(NHL):
                ikv[:, kind * 16 + hl] = (r * 6144 + kind * 2048
                                          + hl * 128 + p32)
        ictxw = np.empty((128, 32), dtype=np.int32)
        for hl in range(NHL):
            for th in range(2):
                ictxw[:, hl * 2 + th] = (th * 4096 + 2048 * r
                                         + 128 * hl + p32)
        ictxr = np.empty((128, 32), dtype=np.int32)
        ix2 = np.empty((128, 32), dtype=np.int32)
        for gc in range(32):
            ictxr[:, gc] = r * 4096 + 128 * gc + p32
            ix2[:, gc] = r * 4096 + 128 * gc + p32
        maps.append(dict(
            xT=xT, xT_own=xT_own, wqk_r=_f32(wqk_r), wv_r=wv_r,
            brow=_f32(brow), w_out_r=w_out_r,
            b_out_row=_f32(b_out.reshape(1, D)), wbT_r=wbT_r,
            b_base_row=_f32(b_base.reshape(1, D)),
            b2_r=b2_r, alibias=_f32(ali), tri=tri, mbr=mbr, ident=ident,
            wr_r=wr_r, a2_r=a2_r,
            ikv=ikv, ictxw=ictxw, ictxr=ictxr, ix2=ix2,
        ))
    return maps


def dyn(ap, off):
    return dataclasses.replace(ap, offset=off)


class Plan:
    ENGS = ("sync", "gpsimd", "tensor", "vector", "scalar")

    def __init__(self):
        self.q = {e: [] for e in self.ENGS}
        self.cnt = {}
        self.hw = {}
        self.H = {}
        self.R = {}

    def emit(self, eng, fn):
        self.q[eng].append(fn)

    def inc(self, sem, amount=16):
        self.cnt[sem] = self.cnt.get(sem, 0) + amount
        return self.cnt[sem]

    def cur(self, sem):
        return self.cnt.get(sem, 0)

    def wait(self, eng, sem, val):
        if val <= 0:
            return
        assert val <= self.cnt.get(sem, 0), (eng, sem, val, self.cnt.get(sem))
        key = (eng, sem)
        if self.hw.get(key, 0) >= val:
            return
        self.hw[key] = val
        H = self.H

        def f(e, sem=sem, val=val):
            e.wait_ge(H[sem], val)
        self.emit(eng, f)


def build_program():
    import os
    STAGE = int(os.environ.get("KSTAGE", "5"))
    nc = bass.Bass("TRN2", target_bir_lowering=False, num_devices=8)
    P_ = Plan()
    H, R = P_.H, P_.R

    def din(name, shape, dtype=F32):
        return nc.dram_tensor(name, list(shape), dtype, kind="ExternalInput").ap()

    d_xT = din("xT", (D, S))
    d_xTo = din("xT_own", (D, T))
    d_wqk = din("wqk_r", (NHL, 128, 32 * 256))
    d_wv = din("wv_r", (8, 128, 32 * 256))
    d_brow = din("brow", (8, 768))
    d_wout = din("w_out_r", (32, 128, D))
    d_bout = din("b_out_row", (1, D))
    d_wbT = din("wbT_r", (32, 128, D))
    d_bbase = din("b_base_row", (1, D))
    d_wr = din("wr_r", (128, 32 * E))
    d_a2 = din("a2_r", (128, D))
    d_b2 = din("b2_r", (32, 128, 128))
    d_ali = din("alibias", (NHL, 128, 256))
    d_tri = din("tri", (128, 128))
    d_mbr = din("mbr", (E, 128))
    d_id = din("ident", (128, 128))
    d_ikv = din("ikv", (128, 48), I32)
    d_ictxw = din("ictxw", (128, 32), I32)
    d_ictxr = din("ictxr", (128, 32), I32)
    d_ix2 = din("ix2", (128, 32), I32)

    d_y = nc.dram_tensor("y", [D, T], F32, kind="ExternalOutput").ap()

    ap_kv = nc.dram_tensor("sh_kv", [2 * 3 * S, S], BF16, kind="Internal",
                           addr_space="Shared").ap()
    ap_ctx = nc.dram_tensor("sh_ctx", [2 * D, T], BF16, kind="Internal",
                            addr_space="Shared").ap()
    ap_x2 = nc.dram_tensor("sh_x2", [2 * D, T], F32, kind="Internal",
                           addr_space="Shared").ap()
    ap_fl = nc.dram_tensor("sh_fl", [1, 64], I32, kind="Internal",
                           addr_space="Shared").ap()

    from contextlib import ExitStack
    es = ExitStack()

    def sb(name, shape, dtype):
        return es.enter_context(nc.sbuf_tensor(name, list(shape), dtype))

    xln = sb("xln", (128, 65536), BF16)
    wv_sb = sb("wv_sb", (128, 8192), BF16)
    wqk_sb = sb("wqk_sb", (128, 8192), BF16)
    xf32 = sb("xf32", (128, 4096), F32)
    tmp32 = sb("tmp32", (128, 1024), F32)
    bcast = sb("bcast", (128, 2048), F32)
    rows = sb("rows", (1, 1024), F32)
    rows_b = sb("rows_b", (1, 3072), BF16)
    alib = sb("alib", (128, 512), F32)
    trit = sb("trit", (128, 128), F32)
    ones_b = sb("ones_b", (128, 512), BF16)
    brow_sb = sb("brow_sb", (1, 1536), BF16)
    brot = sb("brot", (1, 256), BF16)
    ikv_sb = sb("ikv_sb", (128, 48), I32)
    ictxw_sb = sb("ictxw_sb", (128, 32), I32)
    ictxr_sb = sb("ictxr_sb", (128, 32), I32)
    ix2_sb = sb("ix2_sb", (128, 32), I32)
    cst_eps = sb("cst_eps", (128, 1), F32)
    flag_sb = sb("flag_sb", (1, 16), I32)
    pfl_sb = sb("pfl_sb", (1, 64), I32)
    zero_sb = sb("zero_sb", (1, 16), I32)

    xb16 = xln[:, 0:4096]
    sq16 = xln[:, 4096:6144]
    cx = xln[:, 0:32768]
    att = 32768
    qb = [xln[:, att + 0:att + 2048], xln[:, att + 2048:att + 4096]]
    kb = [xln[:, att + 4096:att + 6144], xln[:, att + 6144:att + 8192]]
    vb = [xln[:, att + 8192:att + 10240], xln[:, att + 10240:att + 12288]]
    pt = [xln[:, att + 12288 + 512 * i:att + 12288 + 512 * (i + 1)]
          for i in range(4)]
    ctxo = [xln[:, att + 14336:att + 15360], xln[:, att + 15360:att + 16384]]
    rbf = xln[:, att + 16384:att + 17408].bitcast(F32)
    tritmp = [xln[:, att + 17408 + 256 * i:att + 17408 + 256 * (i + 1)]
              .bitcast(F32) for i in range(4)]
    x2ln = xln[:, 32768:65536]
    x8 = dataclasses.replace(xf32[:, :], offset=0).bitcast(BF16)
    vtmp = [x8[:, 0:2048], x8[:, 2048:4096]]
    qtmp_bufs = [x8[:, 4096:6144], None]
    ktmp_bufs = [x8[:, 6144:8192], None]
    bc16 = dataclasses.replace(bcast[:, :], offset=0).bitcast(BF16)
    qtmp_bufs[1] = bc16[:, 0:2048]
    ktmp_bufs[1] = bc16[:, 2048:4096]
    wo_sb = wv_sb
    w32 = wqk_sb[:, :].bitcast(F32)
    x2q = [w32[:, 0:1024], w32[:, 1024:2048], w32[:, 2048:3072],
           w32[:, 3072:4096]]
    xb2 = xf32[:, :].bitcast(BF16)
    xb2a = xb2[:, 0:1024]
    xb2b = xb2[:, 1024:2048]
    g_sb = xln[:, 0:1024]
    wexp_sb = xln[:, 1024:2048]
    wr_sb = xln[:, 2048:2304]
    id_sb = xln[:, 2304:2560].bitcast(F32)
    mbr_sb = xln[0:8, 2560:2688]
    top = xln[:, 2688:2816].bitcast(F32)
    w8t_sb = xln[0:8, 2816:2944]
    b2rot = [xln[:, 2944:3072], xln[:, 3072:3200]]
    a2_sb = wv_sb[:, 0:4096]

    PS = [es.enter_context(nc.psum_tensor(f"PS{i}", [128, 512], F32))
          for i in range(8)]

    DMASEMS = ("spre", "sxa", "sxb", "sfa", "sfb", "swv", "swq", "skv",
               "skl0", "skl1", "sct0", "sct1", "sfl", "sga", "sxs0", "sxs1",
               "sxl0", "sxl1", "swo0", "swo1", "sia", "sib", "so0", "so1")
    for s in DMASEMS + ("st", "sv", "sa", "sf", "sq", "sp"):
        P_.H[s] = es.enter_context(nc.semaphore(s))


    nc.const_aps.register(F32, EPS, cst_eps[:, 0:1])

    g_, s_, t_, v_, a_ = "gpsimd", "sync", "tensor", "vector", "scalar"
    inc, cur, wait, emit = P_.inc, P_.cur, P_.wait, P_.emit

    bank_guard = [None] * 8

    def pe_take_bank(i):
        if bank_guard[i] is not None:
            wait(t_, *bank_guard[i])

    def consume_bank(i, sem, val):
        bank_guard[i] = (sem, val)

    def p0_regs(g):
        pid = g.partition_id()
        for nm in ("rank", "flw", "flw1"):
            R[nm] = g.alloc_register(nm)
        g.reg_alu(R["rank"], pid, 1, ALU.bitwise_and)
        g.reg_alu(R["flw"], R["rank"], 16, ALU.mult)
        g.reg_alu(R["flw1"], R["flw"], 8, ALU.add)
        g.memset(ones_b[:, :], 1.0)
        g.memset(cst_eps[:, :], EPS)
        g.memset(flag_sb[:, :], MAGIC)
        g.memset(zero_sb[:, :], 0)
    emit(g_, p0_regs)

    def gdma(sem, src_ap, dst_ap):
        v = inc(sem, 16)

        def f(g, src_ap=src_ap, dst_ap=dst_ap, sem=sem):
            g.dma_start(out=dst_ap, in_=src_ap).then_inc(H[sem], 16)
        emit(g_, f)
        return v

    def gdma_dyn(sem, dst_fn, src_fn):
        v = inc(sem, 16)

        def f(g, sem=sem):
            g.dma_start(out=dst_fn(g), in_=src_fn(g)).then_inc(H[sem], 16)
        emit(g_, f)
        return v

    def sdma(sem, src_ap, dst_ap):
        v = inc(sem, 16)

        def f(sy, src_ap=src_ap, dst_ap=dst_ap, sem=sem):
            sy.dma_start(out=dst_ap, in_=src_ap).then_inc(H[sem], 16)
        emit(s_, f)
        return v

    def gscat(sem, dram_ap_full, idx_sb_col, src_slice):
        v = inc(sem, 16)

        def f(g, dram_ap_full=dram_ap_full, idx_sb_col=idx_sb_col,
              src_slice=src_slice, sem=sem):
            from concourse.bass_core_rust import IndirectOffsetOnAxis
            g.indirect_dma_start(
                dram_ap_full, IndirectOffsetOnAxis(ap=idx_sb_col, axis=0),
                src_slice, None).then_inc(H[sem], 16)
        emit(g_, f)
        return v

    def ggath(sem, dst_slice, dram_ap_full, idx_sb_col):
        v = inc(sem, 16)

        def f(g, dram_ap_full=dram_ap_full, idx_sb_col=idx_sb_col,
              dst_slice=dst_slice, sem=sem):
            from concourse.bass_core_rust import IndirectOffsetOnAxis
            g.indirect_dma_start(
                dst_slice, None,
                dram_ap_full, IndirectOffsetOnAxis(ap=idx_sb_col, axis=0),
            ).then_inc(H[sem], 16)
        emit(g_, f)
        return v

    def gchunk(sem, src2d, dst_slice, width):
        return gdma(sem, src2d.rearrange("(c p) f -> p c f", p=128),
                    dst_slice.rearrange("p (c f) -> p c f", f=width))

    gdma("spre", d_tri, trit[:, :])
    gdma("spre", d_ikv, ikv_sb[:, :])
    gdma("spre", d_ictxw, ictxw_sb[:, :])
    gdma("spre", d_ictxr, ictxr_sb[:, :])
    v_pre = gdma("spre", d_ix2, ix2_sb[:, :])
    emit(g_, lambda g: g.nop().then_inc(H["sq"], 1))
    v_cst_sem = "sq"
    v_cst = inc("sq", 1)

    last_mm_for_parity = [0, 0]
    last_sq_for_parity = [0, 0]
    for dc in range(NC):
        pa = dc % 2
        if dc >= 2:
            wait(g_, "st", last_mm_for_parity[pa])
            wait(g_, "sv", last_sq_for_parity[pa])
        vld = gdma("sxa" if pa == 0 else "sxb",
                   d_xT[128 * dc:128 * (dc + 1), :],
                   xb16[:, 2048 * pa:2048 * (pa + 1)])
        wait(v_, "sxa" if pa == 0 else "sxb", vld)
        def fsq(v, pa=pa):
            v.tensor_mul(sq16[:, :], xb16[:, 2048 * pa:2048 * (pa + 1)],
                         xb16[:, 2048 * pa:2048 * (pa + 1)]).then_inc(H["sv"], 1)
        if dc >= 1:
            wait(v_, "st", last_mm_for_parity[1 - pa])
        emit(v_, fsq)
        vsq = inc("sv", 1)
        last_sq_for_parity[pa] = vsq
        wait(t_, "sxa" if pa == 0 else "sxb", vld)
        wait(t_, "sq", v_cst)
        wait(t_, "sv", vsq)
        def fmm(t, dc=dc, pa=pa):
            for ts in range(4):
                t.matmul(PS[ts][0:1, :], ones_b[:, 0:1],
                         xb16[:, 2048 * pa + 512 * ts:2048 * pa + 512 * (ts + 1)],
                         start=(dc == 0), stop=(dc == NC - 1))
            for ts in range(4):
                mm = t.matmul(PS[4 + ts][0:1, :], ones_b[:, 0:1],
                              sq16[:, 512 * ts:512 * (ts + 1)],
                              start=(dc == 0), stop=(dc == NC - 1))
            mm.then_inc(H["st"], 1)
        emit(t_, fmm)
        last_mm_for_parity[pa] = inc("st", 1)
    v_passA_mm = cur("st")

    def ln_stats_and_bcast(sem_ready_val, s1b, s2b, half_cols, nhalfpieces):
        for i, (b1, b2) in enumerate(zip(s1b, s2b)):
            wait(a_, "st", sem_ready_val)
            va0 = inc("sa", 1)
            va = inc("sa", 1)

            def f1(a, b1=b1, b2=b2, i=i, va0=va0):
                a.activation(rows[0:1, 512 * i:512 * (i + 1)],
                             PS[b1][0:1, :], AF.Copy, bias=0.0,
                             scale=1.0 / D).then_inc(H["sa"], 1)
                a.wait_ge(H["sa"], va0)
                a.activation(PS[b1][0:1, :], PS[b2][0:1, :],
                             AF.Copy, bias=0.0, scale=1.0 / D).then_inc(H["sa"], 1)
            emit(a_, f1)
            wait(v_, "sa", va)
            vv0 = inc("sv", 1)
            vv = inc("sv", 1)

            def f2(v, b1=b1, b2=b2, i=i, vv0=vv0):
                v.tensor_mul(tmp32[0:1, 512 * i:512 * (i + 1)],
                             rows[0:1, 512 * i:512 * (i + 1)],
                             rows[0:1, 512 * i:512 * (i + 1)]).then_inc(H["sv"], 1)
                v.wait_ge(H["sv"], vv0)
                v.scalar_tensor_tensor(PS[b2][0:1, :],
                                       tmp32[0:1, 512 * i:512 * (i + 1)],
                                       -1.0, PS[b1][0:1, :],
                                       ALU.mult, ALU.add).then_inc(H["sv"], 1)
            emit(v_, f2)
            wait(a_, "sv", vv)
            vb0 = inc("sa", 1)
            vb1 = inc("sa", 1)
            va2 = inc("sa", 1)

            def f3(a, b1=b1, b2=b2, i=i, vb0=vb0, vb1=vb1):
                a.activation(PS[b1][0:1, :], PS[b2][0:1, :], AF.Ln,
                             bias=EPS, scale=1.0).then_inc(H["sa"], 1)
                a.wait_ge(H["sa"], vb0)
                a.activation(tmp32[0:1, 512 * i:512 * (i + 1)],
                             PS[b1][0:1, :], AF.Exp, bias=0.0,
                             scale=-0.5).then_inc(H["sa"], 1)
                a.wait_ge(H["sa"], vb1)
                a.activation(rows_b[0:1, 512 * i:512 * (i + 1)],
                             tmp32[0:1, 512 * i:512 * (i + 1)],
                             AF.Copy, bias=0.0, scale=1.0).then_inc(H["sa"], 1)
            emit(a_, f3)
            wait(v_, "sa", va2)
            def f4(v, i=i):
                v.tensor_sub(rows_b[0:1, 1024 + 512 * i:1024 + 512 * (i + 1)],
                             tmp32[0:1, 512 * i:512 * (i + 1)],
                             rows_b[0:1, 512 * i:512 * (i + 1)])
                v.tensor_mul(rows_b[0:1, 2048 + 512 * i:2048 + 512 * (i + 1)],
                             rows[0:1, 512 * i:512 * (i + 1)],
                             tmp32[0:1, 512 * i:512 * (i + 1)]).then_inc(H["sv"], 1)
            emit(v_, f4)
            vlo = inc("sv", 1)
            wait(t_, "sv", vlo)
            def f5(t, b1=b1, b2=b2, i=i):
                t.matmul(PS[b1][:, :], ones_b[0:1, 0:128],
                         rows_b[0:1, 512 * i:512 * (i + 1)], start=True, stop=False)
                t.matmul(PS[b1][:, :], ones_b[0:1, 0:128],
                         rows_b[0:1, 1024 + 512 * i:1024 + 512 * (i + 1)],
                         start=False, stop=True)
                t.matmul(PS[b2][:, :], ones_b[0:1, 0:128],
                         rows_b[0:1, 2048 + 512 * i:2048 + 512 * (i + 1)],
                         start=True, stop=True).then_inc(H["st"], 1)
            emit(t_, f5)
            vmm = inc("st", 1)
            wait(a_, "st", vmm)
            def f6(a, b1=b1, b2=b2, i=i):
                a.activation(bcast[:, 512 * i:512 * (i + 1)], PS[b1][:, :],
                             AF.Copy, bias=0.0, scale=1.0)
                a.activation(bcast[:, 1024 + 512 * i:1024 + 512 * (i + 1)],
                             PS[b2][:, :], AF.Copy, bias=0.0,
                             scale=1.0).then_inc(H["sa"], 1)
            emit(a_, f6)
            v_f6 = inc("sa", 1)
            consume_bank(b1, "sa", v_f6)
            consume_bank(b2, "sa", v_f6)
        return cur("sa")

    xln_written = []
    for h in range(2):
        if h == 1:
            wait(a_, "sv", cur("sv"))
            wait(v_, "sv", cur("sv"))
            wait(t_, "sv", cur("sv"))
        ready = ln_stats_and_bcast(v_passA_mm, [2 * h, 2 * h + 1],
                                   [4 + 2 * h, 4 + 2 * h + 1], None, None)
        lastv = [cur("sv"), cur("sv")]
        for dc in range(NC):
            pa = dc % 2
            wait(s_, "sv", lastv[pa])
            vld = sdma("sfa" if pa == 0 else "sfb",
                       d_xT[128 * dc:128 * (dc + 1), 1024 * h:1024 * (h + 1)],
                       xf32[:, 2048 * pa:2048 * pa + 1024])
            wait(v_, "sfa" if pa == 0 else "sfb", vld)
            wait(v_, "sv", cur("sv"))
            wait(v_, "sa", ready)
            vm0 = inc("sv", 1)
            lastv[pa] = inc("sv", 1)

            def fap(v, dc=dc, pa=pa, h=h, vm0=vm0):
                v.tensor_mul(tmp32[:, :],
                             xf32[:, 2048 * pa:2048 * pa + 1024],
                             bcast[:, 0:1024]).then_inc(H["sv"], 1)
                v.wait_ge(H["sv"], vm0)
                v.tensor_sub(xln[:, 2048 * dc + 1024 * h:2048 * dc + 1024 * (h + 1)],
                             tmp32[:, :], bcast[:, 1024:2048]).then_inc(H["sv"], 1)
            emit(v_, fap)
        xln_written.append(cur("sv"))
    v_ln1_done = cur("sv")

    def dump_and_finish(src_getter, nchunks, chunk_cols, src_f32=False):
        for dc in range(nchunks):
            srcap = src_getter(dc)
            for sem in ("sv", "sa", "st") + DMASEMS:
                wait(v_, sem, cur(sem))
            pa = dc % 2
            def fdp(v, srcap=srcap, pa=pa, chunk_cols=chunk_cols):
                v.tensor_copy(x2q[2 + pa][:, 0:chunk_cols],
                              srcap).then_inc(H["sv"], 1)
            emit(v_, fdp)
            vv = inc("sv", 1)
            wait(s_, "sv", vv)
            def fds(sy, dc=dc, pa=pa, chunk_cols=chunk_cols):
                sy.dma_start(out=d_y[128 * dc:128 * (dc + 1), 0:chunk_cols],
                             in_=x2q[2 + pa][:, 0:chunk_cols]).then_inc(H["so0"], 16)
            emit(s_, fds)
            inc("so0", 16)
        for sem in DMASEMS:
            wait(s_, sem, cur(sem))

    if STAGE == 1:
        dump_and_finish(lambda dc: xln[:, 2048 * dc:2048 * dc + 1024], 32, 1024)
        return _assemble(nc, P_, es)

    v_brow = [0] * 8
    v_wv = [0] * 8
    v_wqk = [0] * NHL
    kv_spilled = []
    pe_wv_done = [0] * 8
    pe_wqk_done = [0] * NHL

    v_pair_spill = [0] * 9
    v_qk_spill = [0] * (NHL + 1)
    for hp in range(8):
        if hp >= 1:
            wait(g_, "st", pe_wv_done[hp - 1])
        if hp >= 2:
            wait(g_, "st", pe_wqk_done[2 * (hp - 2) + 1])
        for q4 in range(4):
            v_wv[hp] = gdma("swv", d_wv[hp, :, 2048 * q4:2048 * (q4 + 1)],
                            wv_sb[:, 2048 * q4:2048 * (q4 + 1)])
        v_brow[hp] = gdma("swv", d_brow[hp:hp + 1, :],
                          brow_sb[0:1, 768 * (hp % 2):768 * (hp % 2) + 768])
        wait(t_, "swv", v_brow[hp])
        wait(t_, "sv", v_ln1_done)
        evac_done = [0] * 8
        VBANKS = (0, 1, 4, 5)
        for tt in range(16):
            bk = VBANKS[tt % 4]
            pe_take_bank(bk)
            def fv(t, tt=tt, bk=bk, hp=hp):
                for dc in range(NC):
                    t.matmul(PS[bk][:, 0:256],
                             xln[:, 2048 * dc + 128 * tt:2048 * dc + 128 * (tt + 1)],
                             wv_sb[:, 256 * dc:256 * (dc + 1)],
                             start=(dc == 0), stop=False)
                t.matmul(PS[bk][:, 0:256], ones_b[0:1, 0:128],
                         brow_sb[0:1, 768 * (hp % 2) + 512:768 * (hp % 2) + 768],
                         start=False, stop=True).then_inc(H["st"], 1)
            emit(t_, fv)
            vmm = inc("st", 1)
            wait(a_, "st", vmm)
            if hp >= 1:
                wait(a_, "skv", cur("skv"))
            def fe(a, tt=tt, bk=bk):
                a.activation(vtmp[0][:, 128 * tt:128 * (tt + 1)],
                             PS[bk][:, 0:128], AF.Copy, bias=0.0, scale=1.0)
                a.activation(vtmp[1][:, 128 * tt:128 * (tt + 1)],
                             PS[bk][:, 128:256], AF.Copy, bias=0.0,
                             scale=1.0).then_inc(H["sa"], 1)
            emit(a_, fe)
            va = inc("sa", 1)
            consume_bank(bk, "sa", va)
            evac_done[bk] = va
        pe_wv_done[hp] = cur("st")
        wait(g_, "sa", max(evac_done))
        wait(g_, "spre", v_pre)
        for ho in range(2):
            hl = 2 * hp + ho
            kv_spilled.append(gscat("skv", ap_kv, ikv_sb[:, 32 + hl:33 + hl],
                                    vtmp[ho][:, :]))
        v_pair_spill[hp] = cur("skv")

        for ho in range(2):
            hl = 2 * hp + ho
            if hl >= 1:
                wait(g_, "st", pe_wqk_done[hl - 1])
            for q4 in range(4):
                v_wqk[hl] = gdma(
                    "swq", d_wqk[hl, :, 2048 * q4:2048 * (q4 + 1)],
                    wqk_sb[:, 2048 * q4:2048 * (q4 + 1)])
            wait(t_, "swq", v_wqk[hl])
            for kind, buf, sc, boff in (
                    (0, qtmp_bufs[hl % 2], SCALE_QK, 128 * ho),
                    (1, ktmp_bufs[hl % 2], 1.0, 256 + 128 * ho)):
                ev = 0
                QKBANKS = (2, 3, 6, 7)
                for ts in range(4):
                    bk = QKBANKS[(4 * kind + ts) % 4]
                    pe_take_bank(bk)
                    def fq(t, ts=ts, bk=bk, kind=kind, hp=hp, boff=boff, ho=ho):
                        for dc in range(NC):
                            t.matmul(PS[bk][:, :],
                                     wqk_sb[:, 256 * dc + 128 * kind:
                                            256 * dc + 128 * (kind + 1)],
                                     xln[:, 2048 * dc + 512 * ts:
                                         2048 * dc + 512 * (ts + 1)],
                                     start=(dc == 0), stop=False)
                        t.matmul(PS[bk][:, :],
                                 brow_sb[0:1, 768 * (hp % 2) + boff:
                                         768 * (hp % 2) + boff + 128],
                                 ones_b[0:1, 0:512],
                                 start=False, stop=True).then_inc(H["st"], 1)
                    emit(t_, fq)
                    vmm = inc("st", 1)
                    wait(a_, "st", vmm)
                    if hl >= 1:
                        wait(a_, "skv", cur("skv"))
                    def fqe(a, ts=ts, bk=bk, buf=buf, sc=sc):
                        a.activation(buf[:, 512 * ts:512 * (ts + 1)], PS[bk][:, :],
                                     AF.Copy, bias=0.0, scale=sc).then_inc(H["sa"], 1)
                    emit(a_, fqe)
                    va = inc("sa", 1)
                    consume_bank(bk, "sa", va)
                    ev = va
                wait(g_, "sa", ev)
                kv_spilled.append(gscat(
                    "skv", ap_kv, ikv_sb[:, 16 * kind + hl:16 * kind + hl + 1],
                    buf[:, :]))
            v_qk_spill[hl] = cur("skv")
            pe_wqk_done[hl] = cur("st")
    v_qkv_pe_done = cur("st")
    v_kv_spill_done = cur("skv")

    if STAGE == 2:
        wait(g_, "skv", v_kv_spill_done)
        for hl in range(NHL):
            ggath("sga", cx[:, 2048 * hl:2048 * (hl + 1)], ap_kv,
                  ikv_sb[:, 16 + hl:17 + hl])
        wait(v_, "sga", cur("sga"))
        dump_and_finish(lambda dc: cx[:, 2048 * dc:2048 * dc + 1024], 16, 1024)
        return _assemble(nc, P_, es)

    def phase_barrier():
        for eng in Plan.ENGS:
            for sem in DMASEMS + ("st", "sv", "sa", "sf"):
                if cur(sem):
                    wait(eng, sem, cur(sem))

    v_wo = [0, 0]
    v_brot = [0, 0]
    v_x2in = [0, 0]

    def plan_oproj_loads(oc):
        pa = oc % 2
        sem_wo = "swo0" if pa == 0 else "swo1"
        for q4 in range(4):
            v_wo[pa] = gdma(
                sem_wo, d_wout[oc, :, 1024 * q4:1024 * (q4 + 1)],
                wo_sb[:, 4096 * pa + 1024 * q4:4096 * pa + 1024 * (q4 + 1)])
        v_brot[pa] = gdma(sem_wo, d_bout[0:1, 128 * oc:128 * (oc + 1)],
                          brot[0:1, 128 * pa:128 * (pa + 1)])
        v_x2in[pa] = sdma("sia" if pa == 0 else "sib",
                          d_xTo[128 * oc:128 * (oc + 1), :], x2q[pa])

    wait(g_, "st", pe_wv_done[7])
    wait(g_, "st", pe_wqk_done[NHL - 1])
    wait(s_, "st", pe_wqk_done[NHL - 1])
    plan_oproj_loads(0)
    plan_oproj_loads(1)

    POLLST = {"regs": None}

    def plan_poll(rnd, sfval):
        def fpoll(sy, rnd=rnd):
            if POLLST["regs"] is None:
                POLLST["regs"] = dict(
                    ne=sy.alloc_register("p_ne"), v0=sy.alloc_register("p_v0"),
                    t=sy.alloc_register("p_t"), po=sy.alloc_register("p_po"))
                Rp = POLLST["regs"]
                sy.reg_alu(Rp["po"], sy.partition_id(), 1, ALU.bitwise_and)
                sy.reg_alu(Rp["po"], Rp["po"], 1, ALU.bitwise_xor)
                sy.reg_alu(Rp["po"], Rp["po"], 16, ALU.mult)
                sy.reg_alu(Rp["po"], Rp["po"], 8, ALU.add)
                sy.reg_mov(Rp["t"], 16)
            Rp = POLLST["regs"]
            sy.reg_mov(Rp["ne"], 1)
            with sy.While(Rp["ne"]):
                for _ in range(48):
                    sy.nop()
                sy.dma_start(out=pfl_sb[0:1, 0:8],
                             in_=dyn(ap_fl[0:1, 8:16],
                                     Rp["po"])).then_inc(H["sp"], 16)
                sy.wait_ge(H["sp"], Rp["t"])
                sy.reg_add(Rp["t"], Rp["t"], 16)
                sy.reg_load(Rp["v0"], pfl_sb[0:1, 0:1])
                sy.reg_alu(Rp["ne"], Rp["v0"], MAGIC, ALU.not_equal)
            sy.reg_sub(Rp["t"], Rp["t"], 16)
            sy.wait_ge(H["sp"], Rp["t"])
            sy.dma_start(out=dyn(ap_fl[0:1, 8:16], Rp["po"]),
                         in_=zero_sb[0:1, 0:8]).then_inc(H["sfl"], 16)
            sy.nop().then_inc(H["sf"], 1)
        emit(s_, fpoll)
        inc("sfl", 16)
        assert inc("sf", 1) == sfval

    v_qkvld = [0] * NHL
    pe_head_done = [0] * NHL
    last_scat = 0

    v_alild = [0] * NHL

    def plan_kv_load(hl):
        pa = hl % 2
        sem = "skl0" if pa == 0 else "skl1"
        if hl >= 2:
            wait(g_, "st", pe_head_done[hl - 2])
            wait(g_, "sa", cur("sa"))
        wait(g_, "skv", v_kv_spill_done)
        ncol = NCOL_HL[hl]
        v_alild[hl] = gdma(sem, d_ali[hl, :, 0:ncol],
                           alib[:, 256 * pa:256 * pa + ncol])
        for kind, buf in ((0, qb[pa]), (1, kb[pa]), (2, vb[pa])):
            ggath(sem, buf[:, :], ap_kv,
                  ikv_sb[:, 16 * kind + hl:16 * kind + hl + 1])
        return cur(sem)

    v_qkvld[0] = plan_kv_load(0)
    v_qkvld[1] = plan_kv_load(1)

    pt_consumed = [0, 0, 0, 0]
    ctxo_dma = [0, 0]
    exp_slot_war = {}

    for hl in range(NHL):
        pa = hl % 2
        w = W_HL[hl]
        ng = 2048 // w
        for j in range(4):
            kts = list(range(4 * j + 4))
            blocks = {}
            v_ex_slot = dict(exp_slot_war)

            def plan_sc(kt):
                diag = kt >= 4 * j
                m = kt - 4 * j if diag else 0
                npr = 512 - 128 * m
                c0g = 512 * j + 128 * m
                bk = kt % 4
                pe_take_bank(bk)
                wait(t_, "skl0" if pa == 0 else "skl1", v_qkvld[hl])

                def fsc(t, bk=bk, kt=kt, npr=npr, c0g=c0g, j=j, pa=pa):
                    t.matmul(PS[bk][:, 0:npr],
                             kb[pa][:, 128 * kt:128 * (kt + 1)],
                             qb[pa][:, c0g:512 * (j + 1)],
                             start=True, stop=True).then_inc(H["st"], 1)
                emit(t_, fsc)
                v_sc = inc("st", 1)
                v_tri_v = 0
                if diag:
                    wait(v_, "st", v_sc)
                    wait(v_, "spre", v_pre)
                    if v_ex_slot.get(("tri", bk), 0):
                        wait(v_, "sa", v_ex_slot[("tri", bk)])

                    def ftr(v, bk=bk):
                        v.tensor_add(tritmp[bk][:, 0:128], PS[bk][:, 0:128],
                                     trit[:, :]).then_inc(H["sv"], 1)
                    emit(v_, ftr)
                    v_tri_v = inc("sv", 1)
                wait(a_, "st", max(v_sc, pt_consumed[bk]))
                if diag:
                    wait(a_, "sv", v_tri_v)
                wait(a_, "skl0" if pa == 0 else "skl1", v_qkvld[hl])
                calls = []
                a0 = c0g
                while a0 < 512 * (j + 1):
                    gg = a0 // w
                    b0 = min((gg + 1) * w, 512 * (j + 1))
                    if diag and a0 == c0g:
                        b0 = min(b0, c0g + 128)
                    calls.append((a0, b0, gg))
                    a0 = b0

                def fex(a, calls=calls, bk=bk, diag=diag, c0g=c0g, hl=hl,
                        kt=kt, ng=ng, pa=pa):
                    for (a0, b0, gg) in calls:
                        colb = 256 * pa + kt * ng + gg
                        if diag and a0 == c0g:
                            src_ = tritmp[bk][:, 0:b0 - a0]
                        else:
                            src_ = PS[bk][:, a0 - c0g:b0 - c0g]
                        ai_ = a.activation(pt[bk][:, a0 - c0g:b0 - c0g], src_,
                                           AF.Exp,
                                           bias=alib[:, colb:colb + 1],
                                           scale=1.0)
                    ai_.then_inc(H["sa"], 1)
                emit(a_, fex)
                v_ex = inc("sa", 1)
                consume_bank(bk, "sa", v_ex)
                v_ex_slot[("tri", bk)] = v_ex
                exp_slot_war[("tri", bk)] = v_ex
                blocks[kt] = (v_ex, npr, c0g)

            def plan_cd(kt):
                v_ex, npr, c0g = blocks[kt]
                bk = kt % 4
                if kt == 0:
                    pe_take_bank(4)
                    pe_take_bank(5)
                wait(t_, "sa", v_ex)
                last = (kt == kts[-1])

                def fcd(t, bk=bk, kt=kt, npr=npr, c0g=c0g, j=j, pa=pa,
                        last=last):
                    t.matmul(PS[4][:, c0g - 512 * j:512],
                             vb[pa][:, 128 * kt:128 * (kt + 1)],
                             pt[bk][:, 0:npr],
                             start=(kt == 0), stop=last)
                    t.matmul(PS[5][0:1, c0g - 512 * j:512],
                             ones_b[:, 0:1], pt[bk][:, 0:npr],
                             start=(kt == 0), stop=last).then_inc(H["st"], 1)
                emit(t_, fcd)
                pt_consumed[bk] = inc("st", 1)

            DEPTH = 3
            for i, kt in enumerate(kts):
                plan_sc(kt)
                if i >= DEPTH:
                    plan_cd(kts[i - DEPTH])
            for kt in kts[-DEPTH:] if len(kts) >= DEPTH else kts:
                plan_cd(kt)
            v_cd = cur("st")
            wait(v_, "st", v_cd)
            def fr1(v):
                v.reciprocal(rows[0:1, 0:512], PS[5][0:1, :]).then_inc(H["sv"], 1)
            emit(v_, fr1)
            v_rec = inc("sv", 1)
            consume_bank(5, "sv", v_rec)
            wait(a_, "sv", v_rec)
            def fr2(a):
                a.activation(rows_b[0:1, 0:512], rows[0:1, 0:512], AF.Copy,
                             bias=0.0, scale=1.0).then_inc(H["sa"], 1)
            emit(a_, fr2)
            v_hi = inc("sa", 1)
            wait(v_, "sa", v_hi)
            def fr3(v):
                v.tensor_sub(rows_b[0:1, 512:1024], rows[0:1, 0:512],
                             rows_b[0:1, 0:512]).then_inc(H["sv"], 1)
            emit(v_, fr3)
            v_lo = inc("sv", 1)
            pe_take_bank(6)
            wait(t_, "sv", v_lo)
            def fr4(t):
                t.matmul(PS[6][:, :], ones_b[0:1, 0:128], rows_b[0:1, 0:512],
                         start=True, stop=False)
                t.matmul(PS[6][:, :], ones_b[0:1, 0:128], rows_b[0:1, 512:1024],
                         start=False, stop=True).then_inc(H["st"], 1)
            emit(t_, fr4)
            v_bc = inc("st", 1)
            wait(a_, "st", v_bc)
            def fr5(a):
                a.activation(rbf[:, :], PS[6][:, :], AF.Copy, bias=0.0,
                             scale=1.0).then_inc(H["sa"], 1)
            emit(a_, fr5)
            v_rb = inc("sa", 1)
            consume_bank(6, "sa", v_rb)
            thalf = j // 2
            jh = j % 2
            wait(v_, "sa", v_rb)
            wait(v_, "sct0" if thalf == 0 else "sct1", ctxo_dma[thalf])
            def fr6(v, thalf=thalf, jh=jh):
                v.tensor_mul(ctxo[thalf][:, 512 * jh:512 * (jh + 1)],
                             PS[4][:, :], rbf[:, :]).then_inc(H["sv"], 1)
            emit(v_, fr6)
            v_cm = inc("sv", 1)
            consume_bank(4, "sv", v_cm)
            if jh == 1:
                wait(g_, "sv", v_cm)
                ctxo_dma[thalf] = gscat(
                    "sct0" if thalf == 0 else "sct1", ap_ctx,
                    ictxw_sb[:, hl * 2 + thalf:hl * 2 + thalf + 1],
                    ctxo[thalf][:, :])
        pe_head_done[hl] = cur("st")
        if hl + 2 < NHL:
            v_qkvld[hl + 2] = plan_kv_load(hl + 2)
    wait(g_, "sct0", cur("sct0"))
    wait(g_, "sct1", cur("sct1"))
    v_flag = gdma_dyn(
        "sfl",
        lambda g: dyn(ap_fl[0:1, 8:16], R["flw1"]),
        lambda g: flag_sb[0:1, 0:8])

    plan_poll(1, 1)
    v_poll = cur("sf")

    wait(g_, "sf", v_poll)
    for gc in range(32):
        ggath("sga", cx[:, 1024 * gc:1024 * (gc + 1)], ap_ctx,
              ictxr_sb[:, gc:gc + 1])
    v_gather = cur("sga")

    if STAGE == 3:
        wait(v_, "sga", v_gather)
        dump_and_finish(lambda dc: cx[:, 1024 * dc:1024 * (dc + 1)], 32, 1024)
        return _assemble(nc, P_, es)

    phase_barrier()
    wo_pe_done = [0, 0]
    x2out_dma = [0, 0]
    v_x2add = [0, 0]
    for oc in range(32):
        pa = oc % 2
        sem_wo = "swo0" if pa == 0 else "swo1"
        if oc >= 2:
            wait(g_, "st", wo_pe_done[pa])
            wait(s_, "sv", v_x2add[pa])
            plan_oproj_loads(oc)
        wait(t_, sem_wo, v_brot[pa])
        wait(t_, "sga", v_gather)
        for th in range(2):
            pe_take_bank(th)
            def fop(t, th=th, pa=pa):
                for fc in range(32):
                    t.matmul(PS[th][:, :],
                             wo_sb[:, 4096 * pa + 128 * fc:4096 * pa + 128 * (fc + 1)],
                             cx[:, 1024 * fc + 512 * th:1024 * fc + 512 * (th + 1)],
                             start=(fc == 0), stop=False)
                t.matmul(PS[th][:, :], brot[0:1, 128 * pa:128 * (pa + 1)],
                         ones_b[0:1, 0:512],
                         start=False, stop=True).then_inc(H["st"], 1)
            emit(t_, fop)
            inc("st", 1)
        wo_pe_done[pa] = cur("st")
        wait(v_, "sia" if pa == 0 else "sib", v_x2in[pa])
        wait(v_, "st", wo_pe_done[pa])
        wait(v_, "sxs0" if pa == 0 else "sxs1", x2out_dma[pa])
        def fx2(v, pa=pa):
            v.tensor_add(x2q[2 + pa][:, 0:512], x2q[pa][:, 0:512], PS[0][:, :])
            v.tensor_add(x2q[2 + pa][:, 512:1024], x2q[pa][:, 512:1024],
                         PS[1][:, :]).then_inc(H["sv"], 1)
        emit(v_, fx2)
        v_add = inc("sv", 1)
        v_x2add[pa] = v_add
        consume_bank(0, "sv", v_add)
        consume_bank(1, "sv", v_add)
        wait(a_, "sv", v_add)
        def fcb(a, pa=pa):
            a.activation(xb2a[:, :], x2q[2 + pa][:, :], AF.Copy, bias=0.0,
                         scale=1.0).then_inc(H["sa"], 1)
        emit(a_, fcb)
        v_c1 = inc("sa", 1)
        wait(v_, "sa", v_c1)
        def fsq2(v):
            v.tensor_mul(xb2b[:, :], xb2a[:, :], xb2a[:, :]).then_inc(H["sv"], 1)
        emit(v_, fsq2)
        v_s2 = inc("sv", 1)
        if oc == 0:
            pe_take_bank(4), pe_take_bank(5), pe_take_bank(6), pe_take_bank(7)
        wait(t_, "sv", v_s2)
        def fst(t, oc=oc):
            for th2 in range(2):
                t.matmul(PS[4 + th2][0:1, :], ones_b[:, 0:1],
                         xb2a[:, 512 * th2:512 * (th2 + 1)],
                         start=(oc == 0), stop=(oc == 31))
                mm = t.matmul(PS[6 + th2][0:1, :], ones_b[:, 0:1],
                              xb2b[:, 512 * th2:512 * (th2 + 1)],
                              start=(oc == 0), stop=(oc == 31))
            mm.then_inc(H["st"], 1)
        emit(t_, fst)
        v_stm = inc("st", 1)
        wait(g_, "sv", v_add)
        x2out_dma[pa] = gscat("sxs0" if pa == 0 else "sxs1",
                              ap_x2, ix2_sb[:, oc:oc + 1],
                              x2q[2 + pa][:, :])
    v_x2s0, v_x2s1 = cur("sxs0"), cur("sxs1")
    v_ln2_mm = cur("st")

    wait(a_, "sv", cur("sv"))
    ln_stats_and_bcast(v_ln2_mm, [4, 5], [6, 7], None, None)
    v_ln2_ready = cur("sa")

    x2ln_parts = [0, 0]
    for dc in range(32):
        pa = dc % 2
        wait(g_, "sxs0", v_x2s0)
        wait(g_, "sxs1", v_x2s1)
        if dc >= 2:
            wait(g_, "sv", x2ln_parts[pa])
        vld = ggath("sxl0" if pa == 0 else "sxl1",
                    x2q[pa][:, :], ap_x2, ix2_sb[:, dc:dc + 1])
        wait(v_, "sxl0" if pa == 0 else "sxl1", vld)
        wait(v_, "sv", cur("sv"))
        wait(v_, "sa", v_ln2_ready)
        vm0 = inc("sv", 1)
        x2ln_parts[pa] = inc("sv", 1)

        def fl2(v, dc=dc, pa=pa, vm0=vm0):
            v.tensor_mul(tmp32[:, :], x2q[pa][:, :],
                         bcast[:, 0:1024]).then_inc(H["sv"], 1)
            v.wait_ge(H["sv"], vm0)
            v.tensor_sub(x2ln[:, 1024 * dc:1024 * (dc + 1)], tmp32[:, :],
                         bcast[:, 1024:2048]).then_inc(H["sv"], 1)
        emit(v_, fl2)
    v_x2ln_done = cur("sv")

    if STAGE == 4:
        wait(v_, "sv", v_x2ln_done)
        dump_and_finish(lambda dc: x2ln[:, 1024 * dc:1024 * (dc + 1)], 32, 1024)
        return _assemble(nc, P_, es)

    phase_barrier()
    gdma("spre", d_wr, wr_sb)
    gdma("spre", d_id, id_sb)
    gdma("spre", d_mbr, mbr_sb)
    v_pre4 = gdma("spre", d_a2, a2_sb)

    for tt in range(8):
        pe_take_bank(2)
        wait(t_, "spre", v_pre4)
        wait(t_, "sv", v_x2ln_done)
        def flg(t, tt=tt):
            for dc in range(NC):
                mm = t.matmul(
                    PS[2][:, 0:8],
                    x2ln[:, 1024 * dc + 128 * tt:1024 * dc + 128 * (tt + 1)],
                    wr_sb[:, 8 * dc:8 * (dc + 1)],
                    start=(dc == 0), stop=(dc == NC - 1))
            mm.then_inc(H["st"], 1)
        emit(t_, flg)
        v_lg = inc("st", 1)
        wait(a_, "st", v_lg)
        def fle(a):
            a.activation(top[:, 0:8], PS[2][:, 0:8], AF.Copy, bias=0.0,
                         scale=1.0).then_inc(H["sa"], 1)
        emit(a_, fle)
        v_le = inc("sa", 1)
        consume_bank(2, "sa", v_le)
        wait(v_, "sa", v_le)
        w1 = inc("sv", 1)
        w2 = inc("sv", 1)
        w3 = inc("sv", 1)
        w4 = inc("sv", 1)
        w5 = inc("sv", 1)
        v_t1 = inc("sv", 1)

        def ftp1(v, w1=w1, w2=w2, w3=w3, w4=w4, w5=w5):
            v.reduce_max(top[:, 8:9], top[:, 0:8], AX.X).then_inc(H["sv"], 1)
            v.wait_ge(H["sv"], w1)
            v.tensor_scalar(top[:, 16:24], top[:, 0:8], top[:, 8:9],
                            None, ALU.is_ge).then_inc(H["sv"], 1)
            v.wait_ge(H["sv"], w2)
            v.scalar_tensor_tensor(top[:, 24:32], top[:, 16:24], -1e30,
                                   top[:, 0:8], ALU.mult,
                                   ALU.add).then_inc(H["sv"], 1)
            v.wait_ge(H["sv"], w3)
            v.reduce_max(top[:, 9:10], top[:, 24:32],
                         AX.X).then_inc(H["sv"], 1)
            v.wait_ge(H["sv"], w4)
            v.tensor_scalar(top[:, 32:40], top[:, 24:32], top[:, 9:10], None,
                            ALU.is_ge).then_inc(H["sv"], 1)
            v.wait_ge(H["sv"], w5)
            v.tensor_sub(top[:, 10:11], top[:, 9:10],
                         top[:, 8:9]).then_inc(H["sv"], 1)
        emit(v_, ftp1)
        wait(a_, "sv", v_t1)
        def ftp2(a):
            a.activation(top[:, 11:12], top[:, 10:11], AF.Exp, bias=0.0,
                         scale=1.0).then_inc(H["sa"], 1)
        emit(a_, ftp2)
        v_t2 = inc("sa", 1)
        wait(v_, "sa", v_t2)
        u1 = inc("sv", 1)
        u2 = inc("sv", 1)
        u3 = inc("sv", 1)
        u4 = inc("sv", 1)
        v_t3 = inc("sv", 1)

        def ftp3(v, u1=u1, u2=u2, u3=u3, u4=u4):
            v.tensor_scalar(top[:, 12:13], top[:, 11:12], 1.0, None,
                            ALU.add).then_inc(H["sv"], 1)
            v.wait_ge(H["sv"], u1)
            v.reciprocal(top[:, 13:14], top[:, 12:13]).then_inc(H["sv"], 1)
            v.wait_ge(H["sv"], u2)
            v.tensor_mul(top[:, 14:15], top[:, 11:12],
                         top[:, 13:14]).then_inc(H["sv"], 1)
            v.wait_ge(H["sv"], u3)
            v.tensor_scalar(top[:, 40:48], top[:, 16:24], top[:, 13:14],
                            None, ALU.mult).then_inc(H["sv"], 1)
            v.wait_ge(H["sv"], u4)
            v.scalar_tensor_tensor(top[:, 48:56], top[:, 32:40], top[:, 14:15],
                                   top[:, 40:48], ALU.mult,
                                   ALU.add).then_inc(H["sv"], 1)
        emit(v_, ftp3)
        pe_take_bank(3)
        wait(t_, "sv", v_t3)
        wait(t_, "spre", v_pre4)
        def ftr2(t):
            t.transpose(PS[3][0:8, 0:128], top[:, 48:56],
                        id_sb[:, :]).then_inc(H["st"], 1)
        emit(t_, ftr2)
        v_t4 = inc("st", 1)
        wait(a_, "st", v_t4)
        def fw8(a):
            a.activation(w8t_sb[0:8, :], PS[3][0:8, 0:128], AF.Copy, bias=0.0,
                         scale=1.0).then_inc(H["sa"], 1)
        emit(a_, fw8)
        v_t5 = inc("sa", 1)
        consume_bank(3, "sa", v_t5)
        pe_take_bank(2)
        wait(t_, "sa", v_t5)
        def fwx(t):
            t.matmul(PS[2][:, 0:128], mbr_sb[0:8, :], w8t_sb[0:8, :],
                     start=True, stop=True).then_inc(H["st"], 1)
        emit(t_, fwx)
        v_t6 = inc("st", 1)
        wait(a_, "st", v_t6)
        def fwe(a, tt=tt):
            a.activation(wexp_sb[:, 128 * tt:128 * (tt + 1)], PS[2][:, 0:128],
                         AF.Copy, bias=0.0, scale=1.0).then_inc(H["sa"], 1)
        emit(a_, fwe)
        v_t7 = inc("sa", 1)
        consume_bank(2, "sa", v_t7)
    v_wexp = cur("sa")

    for th in range(2):
        pe_take_bank(0)
        wait(t_, "spre", v_pre4)
        def fh(t, th=th):
            for dc in range(NC):
                mm = t.matmul(
                    PS[0][:, :], a2_sb[:, 128 * dc:128 * (dc + 1)],
                    x2ln[:, 1024 * dc + 512 * th:1024 * dc + 512 * (th + 1)],
                    start=(dc == 0), stop=(dc == NC - 1))
            mm.then_inc(H["st"], 1)
        emit(t_, fh)
        v_h = inc("st", 1)
        wait(v_, "st", v_h)
        wait(v_, "sa", v_wexp)
        def fg(v, th=th):
            v.tensor_mul(g_sb[:, 512 * th:512 * (th + 1)], PS[0][:, :],
                         wexp_sb[:, 512 * th:512 * (th + 1)]).then_inc(H["sv"], 1)
        emit(v_, fg)
        v_g = inc("sv", 1)
        consume_bank(0, "sv", v_g)
    v_g_done = cur("sv")

    fin_pe_done = [0, 0]
    yq_dma = [0, 0]
    v_yadd = [0, 0]
    v_wb = [0, 0]
    v_b2l = [0, 0]
    v_bb = [0, 0]
    v_x2f = [0, 0]
    v_h_pe_done = cur("st")
    for oc in range(32):
        pa = oc % 2
        if oc >= 2:
            wait(g_, "st", fin_pe_done[pa])
        else:
            wait(g_, "st", v_h_pe_done)
        sem_wo = "swo0" if pa == 0 else "swo1"
        if oc >= 2:
            wait(g_, "sv", v_yadd[pa])
        for q4 in range(4):
            v_wb[pa] = gdma(
                sem_wo, d_wbT[oc, :, 1024 * q4:1024 * (q4 + 1)],
                wo_sb[:, 4096 * pa + 1024 * q4:4096 * pa + 1024 * (q4 + 1)])
        v_b2l[pa] = gdma(sem_wo, d_b2[oc], b2rot[pa])
        v_bb[pa] = gdma(sem_wo, d_bbase[0:1, 128 * oc:128 * (oc + 1)],
                        brot[0:1, 128 * pa:128 * (pa + 1)])
        v_x2f[pa] = ggath("sxl0" if pa == 0 else "sxl1",
                          x2q[pa][:, :], ap_x2, ix2_sb[:, oc:oc + 1])
        wait(t_, sem_wo, v_bb[pa])
        wait(t_, "sv", v_g_done)
        for th in range(2):
            pe_take_bank(2 + th)
            def ffi(t, th=th, pa=pa):
                for dc in range(NC):
                    t.matmul(PS[2 + th][:, :],
                             wo_sb[:, 4096 * pa + 128 * dc:
                                   4096 * pa + 128 * (dc + 1)],
                             x2ln[:, 1024 * dc + 512 * th:
                                  1024 * dc + 512 * (th + 1)],
                             start=(dc == 0), stop=False)
                t.matmul(PS[2 + th][:, :], brot[0:1, 128 * pa:128 * (pa + 1)],
                         ones_b[0:1, 0:512], start=False, stop=False)
                t.matmul(PS[2 + th][:, :], b2rot[pa][:, :],
                         g_sb[:, 512 * th:512 * (th + 1)],
                         start=False, stop=True).then_inc(H["st"], 1)
            emit(t_, ffi)
            inc("st", 1)
        fin_pe_done[pa] = cur("st")
        wait(v_, "st", fin_pe_done[pa])
        wait(v_, "sxl0" if pa == 0 else "sxl1", v_x2f[pa])
        wait(v_, "so0" if pa == 0 else "so1", yq_dma[pa])
        def fyo(v, pa=pa):
            v.tensor_add(x2q[2 + pa][:, 0:512], x2q[pa][:, 0:512],
                         PS[2][:, :])
            v.tensor_add(x2q[2 + pa][:, 512:1024], x2q[pa][:, 512:1024],
                         PS[3][:, :]).then_inc(H["sv"], 1)
        emit(v_, fyo)
        v_y = inc("sv", 1)
        v_yadd[pa] = v_y
        consume_bank(2, "sv", v_y)
        consume_bank(3, "sv", v_y)
        wait(s_, "sv", v_y)
        sem_so = "so0" if pa == 0 else "so1"

        def fys(sy, oc=oc, pa=pa, sem_so=sem_so):
            sy.dma_start(out=d_y[128 * oc:128 * (oc + 1), :],
                         in_=x2q[2 + pa][:, :]).then_inc(H[sem_so], 16)
        emit(s_, fys)
        yq_dma[pa] = inc(sem_so, 16)

    for sem in DMASEMS:
        if cur(sem):
            wait(s_, sem, cur(sem))
    return _assemble(nc, P_, es)


def _assemble(nc, P_, es):
    def _run(eng_name):
        def runner(e):
            for i, f in enumerate(P_.q[eng_name]):
                try:
                    f(e)
                except Exception:
                    import traceback
                    print(f"FAILING CLOSURE {eng_name}[{i}] "
                          f"{getattr(f, '__name__', '?')} defined at line "
                          f"{f.__code__.co_firstlineno}")
                    raise
        return runner

    with nc.Block() as block:
        block.sync(_run("sync"))
        block.gpsimd(_run("gpsimd"))
        block.tensor(_run("tensor"))
        block.vector(_run("vector"))
        block.scalar(_run("scalar"))
    es.close()
    return nc


_NC_CACHE = []


def kernel(**inputs):
    maps = make_host_inputs(inputs)
    if not _NC_CACHE:
        _NC_CACHE.append(build_program())
    nc = _NC_CACHE[0]
    res = run_bass_kernel_spmd(nc, maps, core_ids=list(range(8)))
    out = np.empty((B, S, D), dtype=np.float32)
    for c in range(8):
        b, r = c // 2, c % 2
        out[b, r * T:(r + 1) * T, :] = res.results[c]["y"].T
    return out
